# version 12
# baseline (speedup 1.0000x reference)
from contextlib import ExitStack
import numpy as np
import ml_dtypes
import concourse.bass as bass
import concourse.mybir as mybir
from concourse.bass_utils import run_bass_kernel_spmd

F32 = mybir.dt.float32
BF16 = mybir.dt.bfloat16
I32 = mybir.dt.int32
AF = mybir.ActivationFunctionType
AL = mybir.AluOpType

D = 1024
NCH = 8
T = 2048
TT = 512
NTT = 4
HALO = 1
RG = [[0, 1, 2, 3, 4, 5, 6, 7]]
EPS = 1e-5
HID = 2730
NEG = -3.0e38
KVW = 4 * 512 + 4 * 520
HKS = [(i * 128, 128) for i in range(21)] + [(2688, 42)]

_cache = {}
DBG_STOP = None


class Sem:
    def __init__(self, ctx, nc, name):
        self.h = ctx.enter_context(nc.semaphore(name))
        self.n = 0

    def inc(self, inst, k=1):
        inst.then_inc(self.h, k)
        self.n += k
        return inst

    def wait(self, eng):
        eng.wait_ge(self.h, self.n)

    def wait_at(self, eng, val):
        eng.wait_ge(self.h, val)


def ceil_div(a, b):
    return (a + b - 1) // b


def build_nc():
    nc = bass.Bass("TRN2", target_bir_lowering=False, debug=False,
                   detect_race_conditions=True)
    nc.container_num_cores = 8
    ctx = ExitStack()

    def din(name, shape, dtype=F32):
        return nc.dram_tensor(name, shape, dtype, kind="ExternalInput").ap()

    x_fm = din("x_fm", [D, 1 + T])
    WinT = din("WinT", [D, D])
    WoutT = din("WoutT", [D, D])
    b_inp = din("b_inp", [128, NCH])
    b_outp = din("b_outp", [128, NCH])
    scal = din("scal", [1, 8])
    iotapk = din("iotapk", [128, 16])
    rolef = din("rolef", [128, 1])
    rbias = din("rbias", [128, 1])
    sel = din("sel", [1, 1], I32)
    wqkv = din("wqkv", [2, D, 3 * 512])
    wo_i = din("wo_i", [2, 512, D])
    w1_i = din("w1_i", [2, D, 5460])
    w2_i = din("w2_i", [2, HID, D])
    b1p = din("b1p", [2, 128, 44])
    b2p = din("b2p", [2, 128, NCH])
    lgp = din("lgp", [2, 128, NCH])
    lbp = din("lbp", [2, 128, NCH])
    qsv = din("qsv", [2, 1, 64])
    ksv = din("ksv", [2, 1, 64])
    qkpat = din("qkpat", [2, 8, 128, 128])
    cstab4 = din("cstab4", [4, 128, T])
    idmi = din("idmi", [128, 128], BF16)
    utm = din("utm", [128, 128], BF16)
    ltm = din("ltm", [128, 128], BF16)

    out_x2 = nc.dram_tensor("out_x2", [D, T], F32, kind="ExternalOutput").ap()

    att_sc = nc.dram_tensor("att_sc", [D, T], F32, kind="Internal").ap()
    x1_sc = nc.dram_tensor("x1_sc", [D, T], F32, kind="Internal").ap()
    rowb = nc.dram_tensor("rowb", [4, T], F32, kind="Internal").ap()
    rowb_flat = rowb.rearrange("a t -> (a t)")

    ag_in, ag_out = {}, {}

    def agbuf(name, n, dtype=F32):
        ag_in[name] = nc.dram_tensor(f"agi_{name}", [1, n], dtype, kind="Internal").ap()
        ag_out[name] = nc.dram_tensor(f"ago_{name}", [8, n], dtype, kind="Internal").ap()

    agbuf("e0", 128 * 9)
    agbuf("e1", 128 * 8, BF16)
    for l in range(2):
        agbuf(f"e2a_{l}", 128 * 8)
        agbuf(f"e2b_{l}", 128 * KVW, BF16)
        agbuf(f"e3_{l}", 128 * 8)

    sb = ctx.enter_context
    stream = sb(nc.sbuf_tensor("stream", [128, NCH, 1 + T], F32))
    statm = sb(nc.sbuf_tensor("statm", [1, 2 * TT], F32))
    pk = sb(nc.sbuf_tensor("pk", [128, 160], F32))
    small = sb(nc.sbuf_tensor("small", [128, 160], F32))
    smallb = sb(nc.sbuf_tensor("smallb", [128, 32], BF16))
    onesb = sb(nc.sbuf_tensor("onesb", [128, 8], BF16))
    scratch = sb(nc.sbuf_tensor("scratch", [128, 32768], F32))
    recv9 = sb(nc.sbuf_tensor("recv9", [128, 16], F32))
    ssel = sb(nc.sbuf_tensor("ssel", [1, 2], I32))
    rowr = sb(nc.sbuf_tensor("rowr", [65, TT], F32))
    rowrb = sb(nc.sbuf_tensor("rowrb", [65, TT], BF16))
    ones64 = sb(nc.sbuf_tensor("ones64", [65, 64], BF16))
    qkp = sb(nc.sbuf_tensor("qkp", [128, 8, 128], BF16))
    lsm = sb(nc.sbuf_tensor("lsm", [128, 224], F32))
    idm = sb(nc.sbuf_tensor("idm", [128, 128], BF16))
    maskut = sb(nc.sbuf_tensor("maskut", [128, 128], BF16))
    masklt = sb(nc.sbuf_tensor("masklt", [128, 128], BF16))
    bdon = sb(nc.sbuf_tensor("bdon", [128, 2], BF16))
    ckeep = sb(nc.sbuf_tensor("ckeep", [128, NCH], F32))

    def pkc(i, n=1):
        return pk[:, 16 * i:16 * (i + 1 + (n - 1))]

    SL = dict(bin=0, bout=8, scal=16, rolef=24, beta=25, om=26, dlog=27, eps=29,
              rbias=30, zkeep=32, zhal=40, sinb=48, mlast=56)

    def sm(name, n=1, off=0):
        o = SL[name] + off
        return small[:, o:o + n]

    def carve_bf(off, shape):
        flat = int(np.prod(shape[1:]))
        v = scratch[:, off:off + ceil_div(flat, 2)].bitcast(BF16)
        if len(shape) == 3:
            v = v.rearrange("p (a b) -> p a b", a=shape[1])
        return v

    winb = carve_bf(0, [128, NCH, D])
    woutb = carve_bf(4096, [128, NCH, D])
    z2 = [scratch[:, 8192 + i * 2049: 8192 + (i + 1) * 2049] for i in range(2)]
    h2 = [scratch[:, 8192 + i * T: 8192 + (i + 1) * T] for i in range(2)]
    xb_t = carve_bf(12544, [128, 1, TT]).rearrange("p 1 t -> p t")
    sq_t = carve_bf(12800, [128, 1, TT]).rearrange("p 1 t -> p t")
    mcomb_b = carve_bf(13056, [128, 1, T]).rearrange("p 1 t -> p t")
    dp_rep = carve_bf(13056, [128, 1, T]).rearrange("p 1 t -> p t")
    states = carve_bf(14080, [128, NCH, T])
    bfpool = carve_bf(22784, [128, NCH, 1 + T])
    wincs_b = carve_bf(30980, [128, 1, D]).rearrange("p 1 t -> p t")
    xb_t2 = carve_bf(22784, [128, 1, TT]).rearrange("p 1 t -> p t")
    sq_t2 = carve_bf(23040, [128, 1, TT]).rearrange("p 1 t -> p t")
    xb2b = carve_bf(5120, [128, 1, TT]).rearrange("p 1 t -> p t")
    sq2b = carve_bf(5376, [128, 1, TT]).rearrange("p 1 t -> p t")
    attf = [scratch[:, i * TT:(i + 1) * TT] for i in range(2)]
    r_rep = scratch[:, 1024:1024 + T]
    m_rep = scratch[:, 3072:3072 + T]
    t1_t = scratch[:, 5120:5120 + TT]
    mixr_rep = scratch[:, 14080:14080 + T]

    gen2 = attf
    cstab3 = scratch[:, 5632:5632 + 2048].rearrange("p (a b) -> p a b", a=4)
    qkpf_st = scratch[:, 5632:5632 + 1024]
    xb2 = carve_bf(7680, [128, 1, TT]).rearrange("p 1 t -> p t")
    sq2 = carve_bf(7936, [128, 1, TT]).rearrange("p 1 t -> p t")
    rnkb = carve_bf(7936, [128, 1, TT]).rearrange("p 1 t -> p t")
    wqkvb = carve_bf(8192, [128, NCH, 1536])
    wob = carve_bf(14336, [128, 4, 1024])
    qrot = carve_bf(16384, [128, 4, 512])
    qrot0 = carve_bf(1024, [128, 4, 512])
    rcpb = scratch[:, 2048:2560]
    krot = carve_bf(17408, [128, 12, 512])
    vtm = carve_bf(20480, [128, 12, 520])
    khalo = carve_bf(23600, [128, 4, 512])
    vhalo = carve_bf(24624, [128, 4, 520])
    ptil = carve_bf(25664, [128, 8, 512])
    rotst = carve_bf(27712, [128, 8, 512])
    o_b = rotst
    cbw = carve_bf(29760, [128, NCH, 513])
    rnq = t1_t
    rnk = scratch[:, 31812:31812 + TT]
    rnqb = carve_bf(32324, [128, 1, TT]).rearrange("p 1 t -> p t")
    f2b = carve_bf(8192, [128, 11, 1024])
    w2t = carve_bf(16384, [128, 11, 1024])
    w1t = [carve_bf(23600 + i * 1024, [128, NCH, 256]) for i in range(2)]
    gbt = carve_bf(25664, [128, NCH, 1024])

    pbk = [ctx.enter_context(nc.psum_tensor(f"pb{i}", [128, 512], F32)) for i in range(8)]

    s_ldx = Sem(ctx, nc, "ldx")
    s_ldw = Sem(ctx, nc, "ldw")
    s_mm = Sem(ctx, nc, "mm")
    s_act = Sem(ctx, nc, "act")
    s_dve = Sem(ctx, nc, "dve")
    s_gp = Sem(ctx, nc, "gp")
    s_cc = Sem(ctx, nc, "cc")
    s_cci = Sem(ctx, nc, "cci")
    s_row = Sem(ctx, nc, "srow")
    s_st = Sem(ctx, nc, "st")
    s_io = [Sem(ctx, nc, "io0"), Sem(ctx, nc, "io1")]
    s_w2 = Sem(ctx, nc, "w2l")
    s_w1 = [Sem(ctx, nc, "w1a"), Sem(ctx, nc, "w1b")]
    w1_hist = []
    w1_g = [0]
    f2_cnt = [0, 0]
    gel_s = [0, 0, 0]
    w1am_s = [0, 0]
    f2_s = [0, 0, 0]

    block = ctx.enter_context(nc.Block())

    class _BlockProxy:
        def __init__(self, blk, stopped):
            self._blk = blk
            self._stopped = stopped

        def __getattr__(self, name):
            f = getattr(self._blk, name)

            def wrap(fn, *a, **k):
                if self._stopped[0]:
                    return None
                return f(fn, *a, **k)
            return wrap

    _stopflag = [False]
    block = _BlockProxy(block, _stopflag)

    def _ck(tag):
        if DBG_STOP is not None and DBG_STOP == tag:
            _stopflag[0] = True

    _scopes = {}

    def _scope(name):
        _scopes[name] = nc.enter_named_scope(name, False)[0]

    def _escope(name):
        nc.leave_named_scope(name, _scopes[name], False)
    tok = lambda t0, n: slice(HALO + t0, HALO + t0 + n)

    _scope("p01_loads")
    def p0_sync(sync):
        s_ldx.inc(sync.dma_start(out=stream[:, :, :], in_=x_fm.rearrange("(c p) t -> p c t", p=128)), 16)
        s_ldx.inc(sync.dma_start(out=small[:, 0:8], in_=b_inp[:]), 16)
        s_ldx.inc(sync.dma_start(out=small[:, 8:16], in_=b_outp[:]), 16)
        s_ldx.inc(sync.dma_start(out=small[:, 16:24], in_=scal.partition_broadcast(128).rearrange("p 1 x -> p x")), 16)
        s_ldx.inc(sync.dma_start(out=small[:, 24:25], in_=rolef[:]), 16)
        s_ldx.inc(sync.dma_start(out=small[:, 30:31], in_=rbias[:]), 16)
        s_ldx.inc(sync.dma_start(out=pk[:, 144:160], in_=iotapk[:]), 16)
        s_ldx.inc(sync.dma_start(out=ssel[0:1, 0:1], in_=sel[:]), 16)
        s_ldx.inc(sync.dma_start(out=idm[:], in_=idmi[:]), 16)
        s_ldx.inc(sync.dma_start(out=maskut[:], in_=utm[:]), 16)
        s_ldx.inc(sync.dma_start(out=masklt[:], in_=ltm[:]), 16)
    block.sync(p0_sync)
    LD_ALL = s_ldx.n

    def p0_gp(gp):
        s_ldw.inc(gp.dma_start(out=winb[:], in_=WinT.rearrange("(c p) e -> p c e", p=128)), 16)
        s_ldw.inc(gp.dma_start(out=woutb[:], in_=WoutT.rearrange("(c p) e -> p c e", p=128)), 16)
    block.gpsimd(p0_gp)

    def p1_v(v):
        v.memset(onesb[:], 1.0)
        v.memset(small[:, 29:30], float(EPS))
        v.memset(recv9[:], 0.0)
        v.memset(ones64[:], 1.0)
        v.memset(bdon[:], 0.0)
        s_dve.inc(v.memset(smallb[:], 0.0))
        s_dve.wait(v)
        v.memset(bdon[0:64, 0:1], 1.0)
        s_dve.inc(v.memset(bdon[64:128, 1:2], 1.0))
        s_ldx.wait_at(v, LD_ALL)
        v.tensor_scalar(sm("om"), small[:, 17:18], -1.0, 1.0, AL.mult, AL.add)
        s_dve.inc(v.reciprocal(small[:, 28:29], small[:, 17:18]))
        s_dve.wait(v)
        s_dve.inc(v.tensor_mul(sm("beta"), sm("om"), small[:, 28:29]))
    block.vector(p1_v)

    def p1_a(a):
        s_ldx.wait_at(a, LD_ALL)
        s_act.inc(a.activation(sm("dlog"), small[:, 16:17], AF.Ln))
        s_act.wait(a)
        s_act.inc(a.activation(pkc(8), pk[:, 144:160], AF.Exp, scale=sm("dlog")))
    block.scalar(p1_a)

    def ln_stats(src_tile_fn, first_wait_fn, bank, bank2, xbts, sqts,
                 tis=(0, 1, 2, 3), prev_mm_gate=None, prev_dve_gate=None):
        pe0 = s_mm.n
        a0 = s_act.n
        d0 = s_dve.n
        r0 = s_row.n
        nt = len(tis)

        def f_sqs(a):
            for j in range(nt * NCH):
                k, e = divmod(j, NCH)
                if j == 0:
                    first_wait_fn(a)
                if j >= 2:
                    s_mm.wait_at(a, pe0 + j - 1)
                elif prev_mm_gate is not None:
                    s_mm.wait_at(a, prev_mm_gate)
                s_act.inc(a.activation(sqts[j % 2][:], src_tile_fn(e, tis[k]), AF.Square))
        block.scalar(f_sqs)
        assert s_act.n == a0 + nt * NCH

        def f_cst(v):
            for j in range(nt * NCH):
                k, e = divmod(j, NCH)
                if j == 0:
                    first_wait_fn(v)
                if j >= 2:
                    s_mm.wait_at(v, pe0 + j - 1)
                elif prev_mm_gate is not None:
                    s_mm.wait_at(v, prev_mm_gate)
                s_dve.inc(v.tensor_copy(xbts[j % 2][:], src_tile_fn(e, tis[k])))
                if e == NCH - 1:
                    s_mm.wait_at(v, pe0 + NCH * (k + 1))
                    s_row.wait_at(v, r0 + 32 * k)
                    s_dve.inc(v.tensor_copy(statm[0:1, 0:TT], bank[0:1, :]))
                    s_dve.inc(v.tensor_copy(statm[0:1, TT:2 * TT], bank2[0:1, :]))
        block.vector(f_cst)
        assert s_dve.n == d0 + nt * (NCH + 2)

        def f_mms(t):
            for j in range(nt * NCH):
                k, e = divmod(j, NCH)
                s_act.wait_at(t, a0 + j + 1)
                s_dve.wait_at(t, d0 + 10 * k + e + 1)
                if j == 0 and prev_dve_gate is not None:
                    s_dve.wait_at(t, prev_dve_gate)
                t.matmul(bank[0:1, :], onesb[:, 0:1], xbts[j % 2][:],
                         start=(e == 0), stop=(e == NCH - 1))
                s_mm.inc(t.matmul(bank2[0:1, :], onesb[:, 0:1], sqts[j % 2][:],
                                  start=(e == 0), stop=(e == NCH - 1)))
        block.tensor(f_mms)
        assert s_mm.n == pe0 + nt * NCH

        def f_sts(sync):
            for k, tti in enumerate(tis):
                if k == 0:
                    s_row.wait_at(sync, r0)
                s_dve.wait_at(sync, d0 + 10 * k + 10)
                s_row.inc(sync.dma_start(out=rowb[0:1, tti * TT:(tti + 1) * TT],
                                         in_=statm[0:1, 0:TT]), 16)
                s_row.inc(sync.dma_start(out=rowb[1:2, tti * TT:(tti + 1) * TT],
                                         in_=statm[0:1, TT:2 * TT]), 16)
        block.sync(f_sts)
        return s_mm.n, s_dve.n

    def pk_stats():
        def f_g(sync):
            s_row.wait(sync)
            s_row.inc(sync.dma_start(out=pkc(0), in_=rowb[0:1, :].rearrange("1 (p j) -> p j", j=16)), 16)
            s_row.inc(sync.dma_start(out=pkc(1), in_=rowb[1:2, :].rearrange("1 (p j) -> p j", j=16)), 16)
        block.sync(f_g)

        def f_m(v):
            s_row.wait(v)
            v.tensor_scalar_mul(pkc(2), pkc(0), 1.0 / D)
            s_dve.inc(v.tensor_scalar_mul(pkc(3), pkc(1), 1.0 / D))
            s_dve.wait(v)
            s_dve.inc(v.tensor_mul(pkc(4), pkc(2), pkc(2)))
            s_dve.wait(v)
            s_dve.inc(v.tensor_sub(pkc(3), pkc(3), pkc(4)))
        block.vector(f_m)

        def f_a(a):
            s_dve.wait(a)
            s_act.inc(a.activation(pkc(4), pkc(3), AF.Sqrt, bias=sm("eps")))
        block.scalar(f_a)

        def f_m2(v):
            s_act.wait(v)
            s_dve.inc(v.reciprocal(pkc(5), pkc(4)))
            s_dve.wait(v)
            s_dve.inc(v.tensor_mul(pkc(6), pkc(2), pkc(5)))
        block.vector(f_m2)

    def scatter_pk(slot, row):
        def f_s(sync, slot=slot, row=row):
            s_dve.wait(sync)
            s_row.inc(sync.dma_start(out=rowb[row:row + 1, :].rearrange("1 (p j) -> p j", j=16), in_=pkc(slot)), 16)
        block.sync(f_s)

    def replicate(row, dst):
        def f_r(sync, row=row, dst=dst):
            s_row.wait(sync)
            s_row.inc(sync.dma_start(out=dst, in_=rowb[row:row + 1, :].partition_broadcast(128).rearrange("p 1 t -> p t")), 16)
        block.sync(f_r)

    def ag_event(name, stage_ap, recv_ap):
        d_ready = s_dve.n
        g_ready = s_gp.n
        r_ready = s_row.n
        a_ready = s_act.n

        def f_send(g):
            s_dve.wait_at(g, d_ready)
            s_gp.wait_at(g, g_ready)
            s_row.wait_at(g, r_ready)
            s_act.wait_at(g, a_ready)
            k = stage_ap.shape[-1]
            s_cci.inc(g.dma_start(out=ag_in[name][0, :].rearrange("(p c) -> p c", c=k),
                                  in_=stage_ap), 16)
            s_cci.wait(g)
            s_cc.inc(g.collective_compute(
                "AllGather", AL.bypass,
                ins=[ag_in[name][:]],
                outs=[ag_out[name][:]],
                replica_groups=RG,
            ))
        block.gpsimd(f_send)

        def f_recv(sync):
            s_cc.wait(sync)
            slot = sync.value_load(ssel[0:1, 0:1])
            k = recv_ap.shape[-1]
            s_row.inc(sync.dma_start(out=recv_ap,
                                     in_=ag_out[name][bass.ds(slot, 1), :].rearrange("1 (p c) -> p c", c=k)), 16)
        block.sync(f_recv)

    def ln_rep():
        scatter_pk(5, 2)
        scatter_pk(2, 3)

        def f_qg(sync, mprev=s_mm.n):
            s_mm.wait_at(sync, mprev)
        block.sync(f_qg)
        replicate(2, r_rep[:])
        replicate(3, m_rep[:])
        return s_row.n

    _escope("p01_loads")
    _scope("p2_ln1")
    def ln1_first_wait(eng):
        s_ldx.wait_at(eng, LD_ALL)
    ln_stats(lambda e, tti: stream[:, e, tok(tti * TT, TT)], ln1_first_wait,
             pbk[2], pbk[4], (xb_t, xb_t2), (sq_t, sq_t2))
    pk_stats()

    def p2_mixr():
        def f_m(v):
            s_dve.wait(v)
            s_dve.inc(v.tensor_scalar_mul(pkc(7), pkc(5), small[:, 17:18]))
        block.vector(f_m)
        scatter_pk(7, 2)
        replicate(2, mixr_rep[:])
    p2_mixr()

    def p2_mcomb():
        scatter_pk(6, 3)

        def f_s2(sync):
            s_row.wait(sync)
            s_row.inc(sync.dma_start(out=recv9[0:1, 8:9], in_=rowb[3:4, T - 1:T]), 16)
            s_row.inc(sync.dma_start(out=pkc(4), in_=rowb_flat[3 * T - 1:4 * T - 1].rearrange("(p j) -> p j", j=16)), 16)
        block.sync(f_s2)

        def f_m(v):
            s_row.wait(v)
            s_dve.inc(v.tensor_scalar_mul(pkc(9), pkc(6), small[:, 17:18]))
            s_dve.wait(v)
            s_dve.inc(v.scalar_tensor_tensor(pkc(9), pkc(4), sm("om"), pkc(9), AL.mult, AL.add))
        block.vector(f_m)
        scatter_pk(9, 0)

        def f_sc2(g):
            s_row.wait(g)
            s_cci.inc(g.dma_start(out=mcomb_b[0:1, :], in_=rowb[0:1, :]), 16)
        block.gpsimd(f_sc2)
    p2_mcomb()

    def p2_z():
        for e in range(NCH):
            zsl = z2[e % 2]
            gcnt = s_gp.n

            def f_p1(v, e=e, zsl=zsl, gcnt=gcnt):
                s_row.wait(v)
                if e >= 2:
                    s_gp.wait_at(v, gcnt)
                s_dve.inc(v.tensor_mul(zsl[:, 1:1 + T], stream[:, e, tok(0, T)], mixr_rep[:]))
                s_dve.wait(v)
                v.tensor_copy(small[:, 32 + e:33 + e], zsl[:, 1:2])
                s_dve.inc(v.tensor_copy(recv9[:, e:e + 1], zsl[:, 1 + T - 1:1 + T]))
            block.vector(f_p1)
            dcnt_after = s_dve.n

            def f_p2a(v, e=e, zsl=zsl, dcnt=dcnt_after):
                s_dve.wait_at(v, dcnt)
                s_gp.inc(v.scalar_tensor_tensor(bfpool[:, e, HALO + 1:HALO + T],
                                                zsl[:, 1:T], sm("beta"),
                                                zsl[:, 2:1 + T], AL.mult, AL.add))
            block.vector(f_p2a)
    p2_z()

    ag_event("e0", recv9[:, 0:9], recv9[:, 0:9])

    def p2_fix():
        def f_h(v):
            s_row.wait(v)
            s_dve.inc(v.tensor_scalar_mul(small[:, 40:48], recv9[:, 0:8], small[:, 24:25]))
        block.vector(f_h)

        def f_c1(v):
            s_dve.wait(v)
            for e in range(NCH):
                v.scalar_tensor_tensor(bfpool[:, e, HALO:HALO + 1], small[:, 40 + e:41 + e],
                                       sm("beta"), small[:, 32 + e:33 + e], AL.mult, AL.add)
            v.tensor_mul(small[0:1, 56:57], recv9[0:1, 8:9], small[0:1, 24:25])
            s_dve.inc(v.tensor_scalar_mul(small[0:1, 57:58], pk[0:1, 96:97], small[0:1, 17:18]))
            s_dve.wait(v)
            s_dve.inc(v.scalar_tensor_tensor(small[0:1, 58:59], small[0:1, 56:57],
                                             sm("om")[0:1, :], small[0:1, 57:58], AL.mult, AL.add))
        block.vector(f_c1)

        def f_c2(sync):
            s_dve.wait(sync)
            s_cci.wait(sync)
            s_row.inc(sync.dma_start(out=rowb[0:1, 0:1], in_=small[0:1, 58:59]), 16)
        block.sync(f_c2)

        def f_c3(g):
            s_row.wait(g)
            s_cci.inc(g.dma_start(out=mcomb_b[0:1, 0:1], in_=rowb[0:1, 0:1]), 16)
        block.gpsimd(f_c3)
    p2_fix()

    def p2_wincs():
        for ct in range(2):
            acnt = s_act.n

            def f_mm(t, ct=ct, acnt=acnt):
                if ct == 0:
                    s_ldw.wait_at(t, 32)
                else:
                    s_act.wait_at(t, acnt)
                inst = None
                for dch in range(NCH):
                    inst = t.matmul(pbk[3][0:1, :], onesb[:, 0:1],
                                    winb[:, dch, ct * 512:(ct + 1) * 512],
                                    start=(dch == 0), stop=(dch == NCH - 1))
                s_mm.inc(inst)
            block.tensor(f_mm)

            def f_ev(a, ct=ct, mt=s_mm.n):
                s_mm.wait_at(a, mt)
                s_act.inc(a.activation(wincs_b[0:1, ct * 512:(ct + 1) * 512], pbk[3][0:1, :], AF.Copy, scale=-1.0))
            block.scalar(f_ev)
    p2_wincs()

    _escope("p2_ln1")
    _scope("p3_scan")
    inp_d = s_dve.n
    inp_g = s_gp.n
    inp_a = s_act.n
    inp_ci = s_cci.n

    def p3():
        pe0 = s_mm.n
        a0 = s_act.n
        d0 = s_dve.n

        def f_mms(t):
            for e in range(NCH):
                for k, tti in enumerate([1, 2, 3, 0]):
                    i = e * 4 + k
                    bank = pbk[i % 2]
                    if i == 0:
                        s_dve.wait_at(t, inp_d)
                        s_gp.wait_at(t, inp_g)
                        s_act.wait_at(t, inp_a)
                        s_cci.wait_at(t, inp_ci)
                        s_ldw.wait_at(t, 32)
                    if i >= 2:
                        s_act.wait_at(t, a0 + i - 1)
                    for dch in range(NCH):
                        t.matmul(bank[:, :], winb[:, dch, e * 128:(e + 1) * 128],
                                 bfpool[:, dch, 1 + tti * TT: 1 + (tti + 1) * TT],
                                 start=(dch == 0), stop=False)
                    s_mm.inc(t.matmul(bank[:, :], wincs_b[0:1, e * 128:(e + 1) * 128],
                                      mcomb_b[0:1, tti * TT:(tti + 1) * TT],
                                      start=False, stop=True))
        block.tensor(f_mms)
        assert s_mm.n == pe0 + 32

        def f_evs(a):
            for e in range(NCH):
                hsl = h2[e % 2]
                for k, tti in enumerate([1, 2, 3, 0]):
                    i = e * 4 + k
                    s_mm.wait_at(a, pe0 + i + 1)
                    if k == 0 and e >= 2:
                        s_dve.wait_at(a, d0 + 2 * e - 2)
                    s_act.inc(a.activation(hsl[:, tti * TT:(tti + 1) * TT],
                                           pbk[i % 2][:, :], AF.Relu,
                                           bias=small[:, 0 + e:1 + e]))
        block.scalar(f_evs)
        assert s_act.n == a0 + 32

        def f_scans(v):
            for e in range(NCH):
                hsl = h2[e % 2]
                s_act.wait_at(v, a0 + 4 * (e + 1))
                dd0 = small[:, 16:17].broadcast_to([128, T])
                s_dve.inc(v.tensor_tensor_scan(states[:, e, :], dd0, hsl[:, 0:T],
                                               0.0, AL.mult, AL.add))
                s_dve.wait(v)
                s_dve.inc(v.tensor_copy(smallb[:, e:e + 1], states[:, e, T - 1:T]))
        block.vector(f_scans)
        assert s_dve.n == d0 + 16
    p3()
    WIN_MM_DONE = s_mm.n

    ag_event("e1", smallb[:, 0:8], smallb[:, 8:16])

    def p3_dp():
        def f_b(sync):
            s_act.wait_at(sync, 2)
            s_row.inc(sync.dma_start(out=rowb[2:3, :].rearrange("1 (p j) -> p j", j=16), in_=pkc(8)), 16)
        block.sync(f_b)

        def f_b2(g, rt=s_row.n):
            s_row.wait_at(g, rt)
            s_mm.wait_at(g, WIN_MM_DONE)
            s_cci.inc(g.dma_start(out=dp_rep[:], in_=rowb[2:3, :].partition_broadcast(128).rearrange("p 1 t -> p t")), 16)
        block.gpsimd(f_b2)
        ci_t = s_cci.n

        def f_s(v, rt=s_row.n, ci_t=ci_t):
            s_row.wait_at(v, rt)
            s_cci.wait_at(v, ci_t)
            s_dve.inc(v.tensor_scalar_mul(small[:, 48:56], smallb[:, 8:16], small[:, 24:25]))
        block.vector(f_s)

        for e in range(NCH):
            def f_c(v, e=e):
                s_dve.wait(v)
                s_dve.inc(v.scalar_tensor_tensor(states[:, e, :], dp_rep[:], small[:, 48 + e:49 + e],
                                                 states[:, e, :], AL.mult, AL.add))
            block.vector(f_c)
    p3_dp()
    STATES_D = s_dve.n

    _escope("p3_scan")
    _scope("p4_wout")
    def p4():
        pe0 = s_mm.n
        a0 = s_act.n
        d0 = s_dve.n
        r0 = s_row.n
        io0 = (s_io[0].n, s_io[1].n)
        xbs = (xb_t, xb_t2)
        sqs = (sq_t, sq_t2)

        def stats_done(j):
            return pe0 + (2 * j + 3 if j < 31 else 64)

        def f_mms(t):
            def stats(j):
                tti, e = divmod(j, NCH)
                s_act.wait_at(t, a0 + 2 * j + 2)
                s_dve.wait_at(t, d0 + 10 * tti + e + 1)
                t.matmul(pbk[2][0:1, :], onesb[:, 0:1], xbs[j % 2][:],
                         start=(e == 0), stop=(e == NCH - 1))
                s_mm.inc(t.matmul(pbk[4][0:1, :], onesb[:, 0:1], sqs[j % 2][:],
                                  start=(e == 0), stop=(e == NCH - 1)))
            for j in range(32):
                tti, e = divmod(j, NCH)
                if j == 0:
                    s_dve.wait_at(t, STATES_D)
                    s_ldw.wait_at(t, 32)
                if j >= 2:
                    s_act.wait_at(t, a0 + 2 * j - 2)
                inst = None
                for dch in range(NCH):
                    inst = t.matmul(pbk[j % 2][:, :], woutb[:, dch, e * 128:(e + 1) * 128],
                                    states[:, dch, tti * TT:(tti + 1) * TT],
                                    start=(dch == 0), stop=(dch == NCH - 1))
                s_mm.inc(inst)
                if j >= 1:
                    stats(j - 1)
            stats(31)
        block.tensor(f_mms)
        assert s_mm.n == pe0 + 64

        def f_evs(a):
            for j in range(32):
                tti, e = divmod(j, NCH)
                s_mm.wait_at(a, pe0 + (2 * j if j >= 1 else 1))
                if j >= 2:
                    s_io[j % 2].wait_at(a, io0[j % 2] + 16 * (j // 2))
                fsl = attf[j % 2]
                s_act.inc(a.activation(fsl[:], pbk[j % 2][:, :], AF.Identity,
                                       bias=small[:, 8 + e:9 + e]))
                s_act.wait(a)
                s_act.inc(a.activation(sqs[j % 2][:], fsl[:], AF.Square))
        block.scalar(f_evs)
        assert s_act.n == a0 + 64

        def f_casts(v):
            for j in range(32):
                tti = j // NCH
                s_act.wait_at(v, a0 + 2 * j + 1)
                if j >= 2:
                    s_mm.wait_at(v, stats_done(j - 2))
                s_dve.inc(v.tensor_copy(xbs[j % 2][:], attf[j % 2][:]))
                if j % NCH == NCH - 1:
                    s_mm.wait_at(v, stats_done(j))
                    if tti >= 1:
                        s_row.wait_at(v, r0 + 32 * tti)
                    s_dve.inc(v.tensor_copy(statm[0:1, 0:TT], pbk[2][0:1, :]))
                    s_dve.inc(v.tensor_copy(statm[0:1, TT:2 * TT], pbk[4][0:1, :]))
        block.vector(f_casts)
        assert s_dve.n == d0 + 40

        def f_stores(sync):
            for j in range(32):
                tti, e = divmod(j, NCH)
                s_act.wait_at(sync, a0 + 2 * j + 1)
                s_io[j % 2].inc(sync.dma_start(
                    out=att_sc.rearrange("(c p) t -> p c t", p=128)[:, e, tti * TT:(tti + 1) * TT],
                    in_=attf[j % 2][:]), 16)
                if e == NCH - 1:
                    if tti == 0:
                        s_row.wait_at(sync, r0)
                    s_dve.wait_at(sync, d0 + 10 * tti + 10)
                    s_row.inc(sync.dma_start(out=rowb[0:1, tti * TT:(tti + 1) * TT],
                                             in_=statm[0:1, 0:TT]), 16)
                    s_row.inc(sync.dma_start(out=rowb[1:2, tti * TT:(tti + 1) * TT],
                                             in_=statm[0:1, TT:2 * TT]), 16)
        block.sync(f_stores)
    p4()

    pk_stats()
    REP_R = ln_rep()

    def p5():
        wa = [scratch[:, 8192:10240], scratch[:, 10240:12288]]
        d0 = s_dve.n
        g0 = s_gp.n
        io0 = (s_io[0].n, s_io[1].n)

        def sub_cnt(e):
            return d0 + (2 * e if e >= 1 else 1)

        def add_cnt(e):
            return d0 + (2 * e + 3 if e <= 6 else 16)

        def f_lds(sync):
            for e in range(NCH):
                if e == 0:
                    s_io[0].wait_at(sync, io0[0])
                    s_io[1].wait_at(sync, io0[1])
                if e >= 2:
                    s_dve.wait_at(sync, add_cnt(e - 2))
                s_io[e % 2].inc(sync.dma_start(
                    out=wa[e % 2][:],
                    in_=att_sc.rearrange("(c p) t -> p c t", p=128)[:, e, :]), 16)
        block.sync(f_lds)

        def f_v(v):
            for e in range(NCH):
                if e == 0:
                    s_row.wait_at(v, REP_R)
                s_io[e % 2].wait_at(v, io0[e % 2] + 16 * (e // 2 + 1))
                s_dve.inc(v.tensor_sub(wa[e % 2][:], wa[e % 2][:], m_rep[:]))
                if e >= 1:
                    ae = e - 1
                    s_gp.wait_at(v, g0 + ae + 1)
                    s_dve.inc(v.tensor_add(stream[:, ae, HALO:], stream[:, ae, HALO:],
                                           wa[ae % 2][:]))
            s_gp.wait_at(v, g0 + 8)
            s_dve.inc(v.tensor_add(stream[:, 7, HALO:], stream[:, 7, HALO:], wa[1][:]))
        block.vector(f_v)
        assert s_dve.n == d0 + 16

        def f_g(g):
            for e in range(NCH):
                s_dve.wait_at(g, sub_cnt(e))
                s_gp.inc(g.tensor_mul(wa[e % 2][:], wa[e % 2][:], r_rep[:]))
        block.gpsimd(f_g)
        assert s_gp.n == g0 + 8
    p5()

    def p5_store():
        def f_st(sync):
            s_dve.wait(sync)
            s_st.inc(sync.dma_start(out=x1_sc.rearrange("(c p) t -> p c t", p=128), in_=stream[:, :, HALO:]), 16)
        block.sync(f_st)
    p5_store()
    X1_ST = s_st.n

    _escope("p4_wout")
    _scope("p5_ln3")
    def ln3():
        def fw(eng):
            s_dve.wait(eng)
            s_gp.wait(eng)
        ln_stats(lambda e, tti: stream[:, e, tok(tti * TT, TT)], fw, pbk[2], pbk[4], (xb2, xb2b), (sq2, sq2b))
        pk_stats()
        rep3 = ln_rep()

        d0 = s_dve.n
        g0 = s_gp.n

        def f_v3(v, rep3=rep3, x1st=X1_ST):
            for e in range(NCH):
                if e == 0:
                    s_row.wait_at(v, rep3)
                    s_st.wait_at(v, x1st)
                s_dve.inc(v.tensor_sub(stream[:, e, HALO:], stream[:, e, HALO:], m_rep[:]))
        block.vector(f_v3)

        def f_g3(g):
            for e in range(NCH):
                s_dve.wait_at(g, d0 + e + 1)
                s_gp.inc(g.tensor_mul(stream[:, e, HALO:], stream[:, e, HALO:], r_rep[:]))
        block.gpsimd(f_g3)
    ln3()
    _ck("c0")

    _escope("p5_ln3")
    ALIVE = {kc: (max(0, kc - 4), min(3, kc)) for kc in range(8)}

    def kslot(w):
        return 0 if w == 0 else 1 + (w - 1) % 2

    def ffn_layer(l):
        def shsrc(e, tti):
            off = (0 if e >= 4 else 1) + tti * TT
            return stream[:, e, off:off + TT]

        def f_lw(gp, l=l):
            s_mm.wait(gp)
            s_act.wait(gp)
            s_dve.wait(gp)
            s_ldw.inc(gp.dma_start(out=wqkvb[:], in_=wqkv[l].rearrange("(c p) e -> p c e", p=128)), 16)
            s_ldw.inc(gp.dma_start(out=wob[:], in_=wo_i[l].rearrange("(c p) e -> p c e", p=128)), 16)
        block.gpsimd(f_lw)
        LW_T = s_ldw.n

        def f_ls(sync, l=l):
            s_row.wait(sync)
            s_act.wait(sync)
            s_dve.wait(sync)
            s_row.inc(sync.dma_start(out=lsm[:, 0:44], in_=b1p[l]), 16)
            s_row.inc(sync.dma_start(out=lsm[:, 44:52], in_=b2p[l]), 16)
            s_row.inc(sync.dma_start(out=lsm[:, 52:60], in_=lgp[l]), 16)
            s_row.inc(sync.dma_start(out=lsm[:, 60:68], in_=lbp[l]), 16)
            s_row.inc(sync.dma_start(out=lsm[:, 80:144], in_=qsv[l].partition_broadcast(128).rearrange("p 1 x -> p x")), 16)
            s_row.inc(sync.dma_start(out=lsm[:, 144:208], in_=ksv[l].partition_broadcast(128).rearrange("p 1 x -> p x")), 16)
        block.sync(f_ls)

        def f_ls2(sync, l=l, dprev=s_dve.n):
            s_dve.wait_at(sync, dprev)
            s_row.inc(sync.dma_start(out=qkpf_st.rearrange("p (a m) -> p a m", a=8), in_=qkpat[l].rearrange("a p m -> p a m")), 16)
        block.sync(f_ls2)
        LS_T = s_row.n

        def f_cb(v, l=l):
            s_row.wait_at(v, LS_T)
            v.tensor_reduce(lsm[:, 76:77], lsm[:, 80:144], mybir.AxisListType.X, AL.max,
                            apply_absolute_value=True)
            s_dve.inc(v.tensor_reduce(lsm[:, 78:79], lsm[:, 144:208], mybir.AxisListType.X, AL.max,
                                      apply_absolute_value=True))
            s_dve.wait(v)
            s_dve.inc(v.tensor_mul(lsm[:, 76:77], lsm[:, 76:77], lsm[:, 78:79]))
            s_dve.wait(v)
            s_dve.inc(v.tensor_scalar_mul(lsm[:, 76:77], lsm[:, 76:77], -8.0))
            s_dve.wait(v)
            s_dve.inc(v.tensor_add(lsm[:, 77:78], lsm[:, 76:77], sm("rbias")))
            s_dve.inc(v.tensor_copy(qkp[:], qkpf_st.rearrange("p (a m) -> p a m", a=8)))
        block.vector(f_cb)
        CB_T = s_dve.n

        def f_vone(v):
            s_dve.wait(v)
            for sl in range(12):
                v.memset(vtm[:, sl, :].rearrange("p (h c) -> p h c", h=8)[:, :, 64:65], 1.0)
            s_dve.inc(v.memset(vhalo[:, :, :].rearrange("p a (h c) -> p a h c", h=8)[:, :, :, 64:65], 1.0))
        block.vector(f_vone)

        def f_e2a_st(v):
            s_dve.wait(v)
            s_gp.wait(v)
            inst = None
            for e in range(NCH):
                inst = v.tensor_copy(recv9[:, e:e + 1], stream[:, e, HALO + T - 1:HALO + T])
            s_dve.inc(inst)
        block.vector(f_e2a_st)
        ag_event(f"e2a_{l}", recv9[:, 0:8], recv9[:, 0:8])

        def f_e2a_fix(v):
            s_row.wait(v)
            s_dve.inc(v.tensor_scalar_mul(small[:, 40:48], recv9[:, 0:8], small[:, 24:25]))
            s_dve.wait(v)
            inst = None
            for e in range(NCH):
                inst = v.tensor_copy(stream[:, e, 0:1], small[:, 40 + e:41 + e])
            s_dve.inc(inst)
        block.vector(f_e2a_fix)
        HALO_D = s_dve.n

        def keep_col(w):
            def f_k(v, w=w):
                s_dve.wait(v)
                s_gp.wait(v)
                inst = None
                for e in range(NCH):
                    inst = v.tensor_copy(ckeep[:, e:e + 1], stream[:, e, w * TT:w * TT + 1])
                s_dve.inc(inst)
            block.vector(f_k)

        def cast_cb(w):
            def f_c(g, w=w, dh=s_dve.n, mm=s_mm.n):
                s_dve.wait_at(g, dh)
                s_gp.wait(g)
                s_mm.wait_at(g, mm)
                g.tensor_copy(cbw[:, :, 0:1], ckeep[:, :].rearrange("p c -> p c 1") if False else ckeep[:, 0:NCH].rearrange("p (a b) -> p a b", a=NCH))
                s_gp.inc(g.tensor_copy(cbw[:, :, 1:513], stream[:, :, w * TT + 1: w * TT + 513]))
            block.gpsimd(f_c)
            return s_gp.n

        def window_prep(w, cbw_t):
            def f_ct(sync, w=w, dprev=s_dve.n):
                s_dve.wait_at(sync, dprev)
                s_row.inc(sync.dma_start(out=cstab3[:, :, :],
                                         in_=cstab4[:, :, w * TT:(w + 1) * TT].rearrange("a p t -> p a t")), 16)
            block.sync(f_ct)
            CT_T = s_row.n

            def f_nz(v, w=w, mprev=s_mm.n):
                s_mm.wait_at(v, mprev)
                v.memset(pbk[5][:, :], 0.0)
                s_dve.inc(v.memset(pbk[6][:, :], 0.0))
            block.vector(f_nz)
            NZ_T = s_dve.n

            pe0 = s_mm.n
            a0 = s_act.n

            def f_pe1(t, w=w, cbw_t=cbw_t, pe0=pe0, a0=a0, nzt=NZ_T, dprev=s_dve.n):
                def nm(cc):
                    s_act.wait_at(t, a0 + 2 * cc + 2)
                    if cc == 0:
                        s_dve.wait_at(t, nzt)
                    nb = pbk[5] if cc < 4 else pbk[6]
                    base = 32 * (cc % 4)
                    s_mm.inc(t.matmul(nb[base:base + 2, :], bdon[:, :], sq2[:],
                                      start=False, stop=False, skip_group_check=True,
                                      tile_position=(0, base)))
                for cc in range(8):
                    if cc == 0:
                        s_gp.wait_at(t, cbw_t)
                        s_ldw.wait_at(t, LW_T)
                        s_dve.wait_at(t, CB_T)
                        s_dve.wait_at(t, dprev)
                    if cc >= 2:
                        s_act.wait_at(t, a0 + 2 * cc - 2)
                    inst = None
                    for dch in range(NCH):
                        off = 0 if dch >= 4 else 1
                        inst = t.matmul(pbk[cc % 2][:, :], wqkvb[:, dch, cc * 128:(cc + 1) * 128],
                                        cbw[:, dch, off:off + TT],
                                        start=(dch == 0), stop=(dch == NCH - 1))
                    s_mm.inc(inst)
                    if cc >= 1:
                        nm(cc - 1)
                nm(7)
            block.tensor(f_pe1)
            assert s_mm.n == pe0 + 16

            def f_evqs(a, pe0=pe0):
                for cc in range(8):
                    s_mm.wait_at(a, pe0 + (2 * cc + 1 if cc >= 1 else 1))
                    s_act.inc(a.activation(ptil[:, cc, :], pbk[cc % 2][:, :], AF.Copy))
                    s_act.inc(a.activation(sq2[:], pbk[cc % 2][:, :], AF.Square))
            block.scalar(f_evqs)
            assert s_act.n == a0 + 16
            QKEV_T = s_act.n

            def f_nrm(a, nm=pe0 + 16):
                s_mm.wait_at(a, nm)
                s_act.inc(a.activation(rnq[:], pbk[5][:, :], AF.Sqrt))
                s_act.inc(a.activation(rnk[:], pbk[6][:, :], AF.Sqrt))
            block.scalar(f_nrm)

            def f_nrm2(v, at=s_act.n):
                s_act.wait_at(v, at)
                v.tensor_scalar_max(rnq[:], rnq[:], 1e-12)
                s_dve.inc(v.tensor_scalar_max(rnk[:], rnk[:], 1e-12))
                s_dve.wait(v)
                v.reciprocal(rnq[:], rnq[:])
                s_dve.inc(v.reciprocal(rnk[:], rnk[:]))
                s_dve.wait(v)
                v.tensor_copy(rnqb[:], rnq[:])
                s_dve.inc(v.tensor_copy(rnkb[:], rnk[:]))
            block.vector(f_nrm2)
            NRS_T = s_act.n

            pe1 = s_mm.n
            d1 = s_dve.n

            BCB = (pbk[7], pbk[4])

            def f_bcs(t, pe1=pe1, d1=d1, nrs=NRS_T):
                for cc in range(8):
                    if cc == 0:
                        s_act.wait_at(t, nrs)
                    s_dve.wait_at(t, d1 + max(cc - 1, 0))
                    rb = rnqb if cc < 4 else rnkb
                    s_mm.inc(t.matmul(BCB[cc % 2][:, :], qkp[:, cc, :], rb[:], start=True, stop=True))
            block.tensor(f_bcs)
            assert s_mm.n == pe1 + 8

            def f_scs(v, pe1=pe1, qk=QKEV_T):
                for cc in range(8):
                    if cc == 0:
                        s_act.wait_at(v, qk)
                    s_mm.wait_at(v, pe1 + cc + 1)
                    s_dve.inc(v.tensor_mul(ptil[:, cc, :], ptil[:, cc, :], BCB[cc % 2][:, :]))
            block.vector(f_scs)
            assert s_dve.n == d1 + 8
            SC_D = s_dve.n

            def f_rsh(sync, sc=SC_D, d1=d1, wo_done=s_mm.n):
                s_dve.wait_at(sync, d1 + 4)
                s_mm.wait_at(sync, wo_done)
                s_io[0].inc(sync.dma_start(out=rotst[0:32, 0:4, :], in_=ptil[32:64, 0:4, :]), 16)
                s_io[0].inc(sync.dma_start(out=rotst[32:64, 0:4, :], in_=ptil[0:32, 0:4, :]), 16)
                s_io[0].inc(sync.dma_start(out=rotst[64:96, 0:4, :], in_=ptil[96:128, 0:4, :]), 16)
                s_io[0].inc(sync.dma_start(out=rotst[96:128, 0:4, :], in_=ptil[64:96, 0:4, :]), 16)
                s_dve.wait_at(sync, sc)
                s_io[1].inc(sync.dma_start(out=rotst[0:32, 4:8, :], in_=ptil[32:64, 4:8, :]), 16)
                s_io[1].inc(sync.dma_start(out=rotst[32:64, 4:8, :], in_=ptil[0:32, 4:8, :]), 16)
                s_io[1].inc(sync.dma_start(out=rotst[64:96, 4:8, :], in_=ptil[96:128, 4:8, :]), 16)
                s_io[1].inc(sync.dma_start(out=rotst[96:128, 4:8, :], in_=ptil[64:96, 4:8, :]), 16)
            block.sync(f_rsh)
            RSH0 = s_io[0].n
            RSH1 = s_io[1].n

            def f_rcs(v, w=w, rsh0=RSH0, rsh1=RSH1, ct=CT_T):
                for cc in range(8):
                    if cc == 0:
                        s_row.wait_at(v, ct)
                        s_io[0].wait_at(v, rsh0)
                    else:
                        s_dve.wait(v)
                    if cc == 4:
                        s_io[1].wait_at(v, rsh1)
                    else:
                        s_dve.wait(v)
                    ct_c = cstab3[:, 0 if cc < 4 else 2, :]
                    ct_s = cstab3[:, 1 if cc < 4 else 3, :]
                    v.tensor_mul(gen2[0][:], ptil[:, cc, :], ct_c)
                    s_dve.inc(v.tensor_mul(gen2[1][:], rotst[:, cc, :], ct_s))
                    s_dve.wait(v)
                    if cc < 4:
                        dst = (qrot if w else qrot0)[:, cc, :]
                    else:
                        dst = krot[:, kslot(w) * 4 + (cc - 4), :]
                    s_dve.inc(v.tensor_add(dst, gen2[0][:], gen2[1][:]))
            block.vector(f_rcs)
            ROT_D = s_dve.n

            vslot = kslot(w)
            pe2 = s_mm.n
            a2 = s_act.n

            def f_vms(t, cbw_t=cbw_t, pe2=pe2, a2=a2):
                for tb in range(4):
                    if tb == 0:
                        s_gp.wait_at(t, cbw_t)
                    if tb >= 2:
                        s_act.wait_at(t, a2 + tb - 1)
                    inst = None
                    for dch in range(NCH):
                        off = (0 if dch >= 4 else 1) + tb * 128
                        inst = t.matmul(pbk[2 + (tb % 2)][:, :], cbw[:, dch, off:off + 128],
                                        wqkvb[:, dch, 1024:1536],
                                        start=(dch == 0), stop=(dch == NCH - 1))
                    s_mm.inc(inst)
            block.tensor(f_vms)
            assert s_mm.n == pe2 + 4

            def f_ves(a, vslot=vslot, pe2=pe2):
                for tb in range(4):
                    s_mm.wait_at(a, pe2 + tb + 1)
                    dst = vtm[:, vslot * 4 + tb, :].rearrange("p (h c) -> p h c", h=8)[:, :, 0:64]
                    s_act.inc(a.activation(dst, pbk[2 + (tb % 2)][:, :], AF.Copy))
            block.scalar(f_ves)
            assert s_act.n == a2 + 4
            return ROT_D, s_act.n

        def attention(w, use_halo, prep_d, prep_a):
            def f_gate(t, dprev=s_dve.n, prep_a=prep_a):
                s_dve.wait_at(t, dprev)
                s_act.wait_at(t, prep_a)
            block.tensor(f_gate)
            qsrc = qrot0 if use_halo else qrot

            def ksrc_of(h, kc):
                hrow = (h % 2) * 64
                hcc = h // 2
                if kc < 4:
                    if use_halo:
                        return khalo[hrow:hrow + 64, hcc, kc * 128:(kc + 1) * 128]
                    return krot[hrow:hrow + 64, kslot(w - 1) * 4 + hcc, kc * 128:(kc + 1) * 128]
                return krot[hrow:hrow + 64, kslot(w) * 4 + hcc, (kc - 4) * 128:(kc - 3) * 128]

            def vb_of(h, kc, av_m):
                if kc < 4:
                    if use_halo:
                        return vhalo[:, kc, h * 65:h * 65 + av_m]
                    return vtm[:, kslot(w - 1) * 4 + kc, h * 65:h * 65 + av_m]
                return vtm[:, kslot(w) * 4 + (kc - 4), h * 65:h * 65 + av_m]

            def emit_oev(v, hp, rb_cnt):
                s_mm.wait_at(v, rb_cnt)
                s_dve.inc(v.tensor_copy(rcpb[:, :], pbk[7][:, :]))
                s_dve.wait(v)
                v.tensor_mul(o_b[0:64, hp, :], pbk[2][0:64, :], rcpb[0:64, :])
                s_dve.inc(v.tensor_mul(o_b[64:128, hp, :], pbk[3][64:128, :], rcpb[64:128, :]))

            cast_end = [0] * 4
            rb_cnt = [0] * 4
            for hp in range(4):
                he, ho = 2 * hp, 2 * hp + 1
                mm0 = s_mm.n
                a0 = s_act.n
                d0 = s_dve.n
                rbofs = 1 if hp >= 1 else 0
                av_end = mm0 + 40 + rbofs
                if hp >= 1:
                    rb_cnt[hp - 1] = mm0 + 9
                HCFG = ((he, 0, 64, pbk[2], 64, 0), (ho, 64, 64, pbk[3], 0, 1))

                def f_dveA(v, hp=hp, mm0=mm0, d0=d0, prep_a=prep_a, av_end=av_end):
                    if hp >= 1:
                        emit_oev(v, hp - 1, mm0 + 9)
                        s_dve.wait(v)
                    s_mm.wait_at(v, mm0)
                    if hp == 0:
                        s_act.wait_at(v, prep_a)
                        s_dve.wait_at(v, d0)
                    v.memset(pbk[2][0:64, :], 0.0)
                    v.memset(pbk[3][64:128, :], 0.0)
                    v.memset(pbk[5][0:1, :], 0.0)
                    s_dve.inc(v.memset(pbk[5][64:65, :], 0.0))
                    s_mm.wait_at(v, av_end)
                    s_dve.inc(v.reciprocal(rowr[64:65, :], pbk[5][64:65, :]))
                    s_dve.inc(v.reciprocal(rowr[0:1, :], pbk[5][0:1, :]))
                    s_dve.wait(v)
                    v.tensor_copy(rowrb[64:65, :], rowr[64:65, :])
                    s_dve.inc(v.tensor_copy(rowrb[0:1, :], rowr[0:1, :]))
                block.vector(f_dveA)
                assert s_dve.n == d0 + 4 + (2 if hp >= 1 else 0)
                cast_end[hp] = s_dve.n
                zav_cnt = d0 + (2 if hp >= 1 else 0) + 1
                rbofs = 1 if hp >= 1 else 0

                def f_pe(t, hp=hp, mm0=mm0, a0=a0, zav=zav_cnt, HCFG=HCFG, rbofs=rbofs,
                         ce=(cast_end[hp - 1] if hp >= 1 else 0)):
                    for h, av_base, av_m, avbank, sumrow, eo in HCFG:
                        hrow = (h % 2) * 64
                        hcc = h // 2
                        for kc in range(8):
                            qb0, qb1 = ALIVE[kc]
                            nq = (qb1 - qb0 + 1) * 128
                            q0 = qb0 * 128
                            s_act.wait_at(t, a0 + eo * 8 + kc - 1)
                            if kc >= 4:
                                mtile = (maskut, (kc - 4) * 128)
                            else:
                                mtile = (masklt, kc * 128)
                            t.matmul(pbk[kc % 2][0:128, q0:q0 + nq], ksrc_of(h, kc),
                                     qsrc[hrow:hrow + 64, hcc, q0:q0 + nq],
                                     start=True, stop=False)
                            s_mm.inc(t.matmul(pbk[kc % 2][0:128, mtile[1]:mtile[1] + 128],
                                              idm[:], mtile[0][:],
                                              start=False, stop=True))
                        if eo == 0 and hp >= 1:
                            s_dve.wait_at(t, ce)
                            t.matmul(pbk[7][0:64, :], ones64[64:65, 0:64], rowrb[64:65, :],
                                     start=True, stop=True)
                            s_mm.inc(t.matmul(pbk[7][64:128, :], ones64[0:1, 0:64], rowrb[0:1, :],
                                              start=True, stop=False, skip_group_check=True,
                                              tile_position=(0, 64)))
                        for kc in range(8):
                            qb0, qb1 = ALIVE[kc]
                            nq = (qb1 - qb0 + 1) * 128
                            q0 = qb0 * 128
                            if eo == 0 and kc == 0:
                                s_dve.wait_at(t, zav)
                            s_act.wait_at(t, a0 + eo * 8 + kc + 1)
                            s_mm.inc(t.matmul(pbk[5][sumrow:sumrow + 1, q0:q0 + nq],
                                              onesb[:, 0:1], ptil[:, kc, q0:q0 + nq],
                                              start=False, stop=False, skip_group_check=True,
                                              tile_position=(0, sumrow)))
                        for kc in range(8):
                            qb0, qb1 = ALIVE[kc]
                            nq = (qb1 - qb0 + 1) * 128
                            q0 = qb0 * 128
                            s_mm.inc(t.matmul(avbank[av_base:av_base + av_m, q0:q0 + nq],
                                              vb_of(h, kc, av_m), ptil[:, kc, q0:q0 + nq],
                                              start=False, stop=False, skip_group_check=True))
                block.tensor(f_pe)
                assert s_mm.n == mm0 + 48 + (1 if hp >= 1 else 0)

                def f_exps(a, mm0=mm0, HCFG=HCFG, rbofs=rbofs):
                    for h, av_base, av_m, avbank, sumrow, eo in HCFG:
                        for kc in range(8):
                            qb0, qb1 = ALIVE[kc]
                            nq = (qb1 - qb0 + 1) * 128
                            q0 = qb0 * 128
                            if eo == 0:
                                s_mm.wait_at(a, mm0 + kc + 1)
                            else:
                                s_mm.wait_at(a, mm0 + rbofs + 24 + kc + 1)
                            bias_ap = lsm[:, 77:78] if (kc < 4 and use_halo) else lsm[:, 76:77]
                            s_act.inc(a.activation(ptil[:, kc, q0:q0 + nq],
                                                   pbk[kc % 2][0:128, q0:q0 + nq],
                                                   AF.Exp, scale=8.0, bias=bias_ap))
                block.scalar(f_exps)
                assert s_act.n == a0 + 16

            def f_rb3(t, ce=cast_end[3]):
                s_dve.wait_at(t, ce)
                t.matmul(pbk[7][0:64, :], ones64[64:65, 0:64], rowrb[64:65, :],
                         start=True, stop=True)
                s_mm.inc(t.matmul(pbk[7][64:128, :], ones64[0:1, 0:64], rowrb[0:1, :],
                                  start=True, stop=False, skip_group_check=True,
                                  tile_position=(0, 64)))
            block.tensor(f_rb3)
            RB3 = s_mm.n

            def f_dveZ(v, rb3=RB3):
                emit_oev(v, 3, rb3)
            block.vector(f_dveZ)
            return s_dve.n

        def wo_c1(w, o_done):
            pe0 = s_mm.n
            d0 = s_dve.n

            def f_wms(t, o_done=o_done, pe0=pe0, d0=d0):
                for e in range(NCH):
                    if e == 0:
                        s_dve.wait_at(t, o_done)
                    if e >= 2:
                        s_dve.wait_at(t, d0 + e - 1)
                    inst = None
                    for hp in range(4):
                        inst = t.matmul(pbk[e % 2][:, :], wob[:, hp, e * 128:(e + 1) * 128],
                                        o_b[:, hp, :],
                                        start=(hp == 0), stop=(hp == 3))
                    s_mm.inc(inst)
            block.tensor(f_wms)
            assert s_mm.n == pe0 + 8

            def f_c1s(v, w=w, pe0=pe0):
                for e in range(NCH):
                    s_mm.wait_at(v, pe0 + e + 1)
                    s_dve.inc(v.tensor_add(stream[:, e, tok(w * TT, TT)],
                                           stream[:, e, tok(w * TT, TT)], pbk[e % 2][:, :]))
            block.vector(f_c1s)
            assert s_dve.n == d0 + 8
            return s_dve.n

        _scope(f"L{l}_win01")
        keep_col(0)
        cb_t = cast_cb(0)
        pd, pa = window_prep(0, cb_t)
        _ck((l, "prep", 0))
        keep_col(1)
        cb_t = cast_cb(1)
        pd, pa = window_prep(1, cb_t)
        _ck((l, "prep", 1))
        _escope(f"L{l}_win01")
        _scope(f"L{l}_win23")
        keep_col(2)
        cb_t = cast_cb(2)
        od = attention(1, False, pd, pa)
        _ck((l, "attn", 1))
        wo_c1(1, od)
        pd, pa = window_prep(2, cb_t)
        keep_col(3)
        cb_t = cast_cb(3)
        od = attention(2, False, pd, pa)
        _ck((l, "attn", 2))
        wo_c1(2, od)
        pd, pa = window_prep(3, cb_t)
        sl3 = kslot(3)

        def f_e2b(g, sl3=sl3, l=l):
            s_dve.wait(g)
            s_gp.wait(g)
            s_act.wait(g)
            s_cci.inc(g.dma_start(out=ag_in[f"e2b_{l}"][0, :].rearrange("(p c) -> p c", c=KVW)[:, 0:4 * 512],
                                  in_=krot[:, sl3 * 4:(sl3 + 1) * 4, :].rearrange("p a t -> p (a t)")), 16)
            s_cci.inc(g.dma_start(out=ag_in[f"e2b_{l}"][0, :].rearrange("(p c) -> p c", c=KVW)[:, 4 * 512:],
                                  in_=vtm[:, sl3 * 4:(sl3 + 1) * 4, :].rearrange("p a t -> p (a t)")), 16)
            s_cci.wait(g)
            s_cc.inc(g.collective_compute(
                "AllGather", AL.bypass,
                ins=[ag_in[f"e2b_{l}"][:]],
                outs=[ag_out[f"e2b_{l}"][:]],
                replica_groups=RG,
            ))
        block.gpsimd(f_e2b)

        def f_e2br(sync, l=l):
            s_cc.wait(sync)
            slot = sync.value_load(ssel[0:1, 0:1])
            src = ag_out[f"e2b_{l}"][bass.ds(slot, 1), :].rearrange("1 (p c) -> p c", c=KVW)
            s_row.inc(sync.dma_start(out=khalo[:].rearrange("p a t -> p (a t)"), in_=src[:, 0:4 * 512]), 16)
            s_row.inc(sync.dma_start(out=vhalo[:].rearrange("p a t -> p (a t)"), in_=src[:, 4 * 512:]), 16)
        block.sync(f_e2br)
        E2B_R = s_row.n

        od = attention(3, False, pd, pa)
        _ck((l, "attn", 3))
        wo_c1(3, od)
        _escope(f"L{l}_win23")
        _scope(f"L{l}_halo")

        def fw4a(eng, dprev=s_dve.n):
            s_dve.wait_at(eng, dprev)
        ln4A = ln_stats(shsrc, fw4a, pbk[4], pbk[6], (xb2, xb2b), (sq2, sq2b),
                        tis=(2, 3))

        def f_gate0(t, rt=E2B_R):
            s_row.wait_at(t, rt)
        block.tensor(f_gate0)
        od = attention(0, True, pd, pa)
        _ck((l, "attn", 0))
        wo_c1(0, od)
        _ck((l, "c1pre"))

        def f_e3_st(v):
            s_dve.wait(v)
            inst = None
            for e in range(NCH):
                inst = v.tensor_copy(recv9[:, e:e + 1], stream[:, e, HALO + T - 1:HALO + T])
            s_dve.inc(inst)
        block.vector(f_e3_st)
        ag_event(f"e3_{l}", recv9[:, 0:8], recv9[:, 0:8])

        def f_e3_fix(v):
            s_row.wait(v)
            s_dve.inc(v.tensor_scalar_mul(small[:, 40:48], recv9[:, 0:8], small[:, 24:25]))
            s_dve.wait(v)
            inst = None
            for e in range(NCH):
                inst = v.tensor_copy(stream[:, e, 0:1], small[:, 40 + e:41 + e])
            s_dve.inc(inst)
        block.vector(f_e3_fix)
        _ck((l, "c1"))
        _escope(f"L{l}_halo")
        _scope(f"L{l}_ln4")

        def fw4(eng):
            s_dve.wait(eng)
        ln_stats(shsrc, fw4, pbk[4], pbk[6], (xb2, xb2b), (sq2, sq2b),
                 tis=(1, 0), prev_mm_gate=ln4A[0], prev_dve_gate=ln4A[1])
        pk_stats()
        rep4 = ln_rep()

        _escope(f"L{l}_ln4")
        _scope(f"L{l}_w1w2")
        for tti2 in range(2):
            d0g = s_dve.n
            a0g = s_act.n
            mulc = []

            def f_gbv(v, tti2=tti2, rep4=rep4, a0g=a0g):
                j = 0
                for e in range(NCH):
                    for sub in range(2):
                        tti = tti2 * 2 + sub
                        g = gen2[j % 2]
                        if j == 0:
                            s_row.wait_at(v, rep4)
                            s_act.wait_at(v, a0g)
                        if j >= 2:
                            s_act.wait_at(v, a0g + j - 1)
                        s_dve.inc(v.tensor_sub(g[:], shsrc(e, tti),
                                               m_rep[:, tti * TT:(tti + 1) * TT]))
                        if tti == 2 and e >= 4:
                            s_dve.wait(v)
                            s_dve.inc(v.tensor_sub(g[:, 0:1], ckeep[:, e:e + 1],
                                                   m_rep[:, tti * TT:tti * TT + 1]))
                        s_dve.wait(v)
                        s_dve.inc(v.tensor_mul(g[:], g[:], r_rep[:, tti * TT:(tti + 1) * TT]))
                        mulc.append(s_dve.n)
                        j += 1
            block.vector(f_gbv)

            def f_gbs(a, mulc=mulc, a0g=a0g):
                j = 0
                for e in range(NCH):
                    for sub in range(2):
                        s_dve.wait_at(a, mulc[j])
                        s_act.inc(a.activation(gbt[:, e, sub * TT:(sub + 1) * TT],
                                               gen2[j % 2][:], AF.Identity,
                                               bias=lsm[:, 60 + e:61 + e],
                                               scale=lsm[:, 52 + e:53 + e]))
                        j += 1
            block.scalar(f_gbs)
            assert s_act.n == a0g + 16

            if tti2 == 0:
                def f_sv(v, dprev=s_dve.n):
                    s_dve.wait_at(v, dprev)
                    s_gp.wait(v)
                    inst = None
                    for e in range(4, NCH):
                        inst = v.tensor_copy(ckeep[:, e:e + 1], stream[:, e, 1024:1025])
                    s_dve.inc(inst)
                block.vector(f_sv)
            GB_T = s_act.n
            _ck((l, "g", tti2))
            for kh in range(2):
                hks = HKS[kh * 11:(kh + 1) * 11]
                c0 = kh * 11 * 128
                ncols = min(11 * 128, HID - c0)
                pe0 = s_mm.n
                a0 = s_act.n
                d0 = s_dve.n
                w1l_t = []

                ki_slot = [(w1_g[0] + ki) % 2 for ki in range(11)]
                w1_g[0] += 11

                def f_gld(g, l=l, c0=c0, ncols=ncols, hks=hks, pe0=pe0, mprev=s_mm.n,
                          ams=tuple(w1am_s), ki_slot=ki_slot,
                          fresh=(tti2 == 0 and kh == 0)):
                    def w1load(ki, hk0, hkn):
                        slot = ki_slot[ki]
                        slsem = s_w1[slot]
                        wsl = w1t[slot]
                        if ki >= 2:
                            s_mm.wait_at(g, pe0 + 2 * ki - 2)
                        elif fresh:
                            s_mm.wait_at(g, mprev)
                        else:
                            s_mm.wait_at(g, ams[slot])
                        slsem.inc(g.dma_start(out=wsl[:, :, 0:hkn],
                                              in_=w1_i[l][:, hk0:hk0 + hkn].rearrange("(c p) e -> p c e", p=128)), 16)
                        slsem.inc(g.dma_start(out=wsl[:, :, 128:128 + hkn],
                                              in_=w1_i[l][:, HID + hk0:HID + hk0 + hkn].rearrange("(c p) e -> p c e", p=128)), 16)
                        w1l_t.append((slsem, slsem.n, wsl))
                    for ki in (0, 1):
                        w1load(ki, hks[ki][0], hks[ki][1])
                    s_mm.wait_at(g, mprev)
                    nfull = ncols // 128
                    rem = ncols - nfull * 128
                    s_w2.inc(g.dma_start(out=w2t[:, 0:nfull, :],
                                         in_=w2_i[l][c0:c0 + nfull * 128, :].rearrange("(c p) e -> p c e", p=128)), 16)
                    if rem:
                        s_w2.inc(g.dma_start(out=w2t[0:rem, nfull, :],
                                             in_=w2_i[l][c0 + nfull * 128:c0 + ncols, :]), 16)
                    for ki in range(2, 11):
                        w1load(ki, hks[ki][0], hks[ki][1])
                block.gpsimd(f_gld)
                W2L_T = s_w2.n

                ABK = (pbk[0], pbk[1], pbk[6])
                GBK = (pbk[2], pbk[3], pbk[7])

                def f_ams(t, hks=hks, pe0=pe0, a0=a0, d0=d0, gb=GB_T, w1l_t=w1l_t,
                          gs=tuple(gel_s), fs=tuple(f2_s)):
                    for ki, (hk0, hkn) in enumerate(hks):
                        slsem, w1l, wsl = w1l_t[ki]
                        for half2 in range(2):
                            j = 2 * ki + half2
                            if half2 == 0:
                                slsem.wait_at(t, w1l)
                                if ki == 0:
                                    s_act.wait_at(t, gb)
                            if j >= 3:
                                s_act.wait_at(t, a0 + j - 2)
                                s_dve.wait_at(t, d0 + j - 2)
                            else:
                                s_act.wait_at(t, gs[j % 3])
                                s_dve.wait_at(t, fs[j % 3])
                            hc = slice(half2 * 512, (half2 + 1) * 512)
                            for dch in range(NCH):
                                t.matmul(ABK[j % 3][0:hkn, :], wsl[:, dch, 0:hkn],
                                         gbt[:, dch, hc], start=(dch == 0), stop=(dch == NCH - 1))
                            inst = None
                            for dch in range(NCH):
                                inst = t.matmul(GBK[j % 3][0:hkn, :], wsl[:, dch, 128:128 + hkn],
                                                gbt[:, dch, hc], start=(dch == 0), stop=(dch == NCH - 1))
                            s_mm.inc(inst)
                block.tensor(f_ams)
                assert s_mm.n == pe0 + 22

                def f_gels(a, hks=hks, kh=kh, pe0=pe0, d0=d0,
                           f2p=(f2_cnt[0], f2_cnt[1])):
                    for ki, (hk0, hkn) in enumerate(hks):
                        for half2 in range(2):
                            j = 2 * ki + half2
                            s_mm.wait_at(a, pe0 + j + 1)
                            if ki == 0:
                                s_dve.wait_at(a, f2p[half2])
                            else:
                                s_dve.wait_at(a, d0 + j - 1)
                            bcol = 22 + kh * 11 + ki
                            sqh = (xb2, sq2)[half2]
                            s_act.inc(a.activation(sqh[0:hkn, :], GBK[j % 3][0:hkn, :],
                                                   AF.Gelu, bias=lsm[0:hkn, bcol:bcol + 1]))
                block.scalar(f_gels)
                assert s_act.n == a0 + 22

                def f_f2s(v, hks=hks, kh=kh, a0=a0):
                    for ki, (hk0, hkn) in enumerate(hks):
                        for half2 in range(2):
                            j = 2 * ki + half2
                            s_act.wait_at(v, a0 + j + 1)
                            acol = kh * 11 + ki
                            sqh = (xb2, sq2)[half2]
                            s_dve.inc(v.scalar_tensor_tensor(
                                f2b[0:hkn, ki, half2 * 512:(half2 + 1) * 512],
                                ABK[j % 3][0:hkn, :], lsm[0:hkn, acol:acol + 1],
                                sqh[0:hkn, :], AL.add, AL.mult))
                block.vector(f_f2s)
                assert s_dve.n == d0 + 22
                f2_cnt[0] = d0 + 21
                f2_cnt[1] = d0 + 22
                for _s in range(3):
                    _lastj = max(j2 for j2 in range(22) if j2 % 3 == _s)
                    gel_s[_s] = a0 + _lastj + 1
                    f2_s[_s] = d0 + _lastj + 1
                for _ki in range(11):
                    w1am_s[ki_slot[_ki]] = pe0 + 2 * _ki + 2
                F2_T = s_dve.n

                pec = s_mm.n
                dc0 = s_dve.n

                def f_c2ms(t, hks=hks, pec=pec, dc0=dc0, f2t=F2_T, w2l=W2L_T):
                    for idx in range(16):
                        e, half2 = idx // 2, idx % 2
                        if idx == 0:
                            s_dve.wait_at(t, f2t)
                            s_w2.wait_at(t, w2l)
                        if idx >= 2:
                            s_dve.wait_at(t, dc0 + idx - 1)
                        inst = None
                        for ki, (hk0, hkn) in enumerate(hks):
                            inst = t.matmul(pbk[4 + (idx % 2)][:, :],
                                            w2t[0:hkn, ki, e * 128:(e + 1) * 128],
                                            f2b[0:hkn, ki, half2 * 512:(half2 + 1) * 512],
                                            start=(ki == 0), stop=(ki == len(hks) - 1))
                        s_mm.inc(inst)
                block.tensor(f_c2ms)
                assert s_mm.n == pec + 16

                def f_c2es(v, tti2=tti2, kh=kh, pec=pec):
                    for idx in range(16):
                        e, half2 = idx // 2, idx % 2
                        s_mm.wait_at(v, pec + idx + 1)
                        csl = tok(tti2 * 1024 + half2 * 512, TT)
                        if kh == 0:
                            s_dve.inc(v.scalar_tensor_tensor(stream[:, e, csl], pbk[4 + (idx % 2)][:, :],
                                                             lsm[:, 44 + e:45 + e],
                                                             stream[:, e, csl], AL.add, AL.add))
                        else:
                            s_dve.inc(v.tensor_add(stream[:, e, csl], stream[:, e, csl],
                                                   pbk[4 + (idx % 2)][:, :]))
                block.vector(f_c2es)
                assert s_dve.n == dc0 + 16
        _escope(f"L{l}_w1w2")

    ffn_layer(0)
    _ck((0, "c2"))
    ffn_layer(1)
    _ck((1, "c2"))

    _scope("ln5")
    def ln5():
        def fw(eng):
            s_dve.wait(eng)
        ln_stats(lambda e, tti: stream[:, e, tok(tti * TT, TT)], fw, pbk[2], pbk[4], (xb2, xb2b), (sq2, sq2b))
        pk_stats()
        rep5 = ln_rep()

        x15 = [scratch[:, 8192:10240], scratch[:, 10240:12288]]
        d0 = s_dve.n
        g0 = s_gp.n
        io0 = (s_io[0].n, s_io[1].n)

        def sub_cnt(e):
            return d0 + (2 * e if e >= 1 else 1)

        def add_cnt(e):
            return d0 + (2 * e + 3 if e <= 6 else 16)

        def f_lds(sync, rep5=rep5):
            for e in range(NCH):
                if e == 0:
                    s_st.wait_at(sync, X1_ST)
                if e >= 2:
                    s_dve.wait_at(sync, add_cnt(e - 2))
                s_io[e % 2].inc(sync.dma_start(
                    out=x15[e % 2][:],
                    in_=x1_sc.rearrange("(c p) t -> p c t", p=128)[:, e, :]), 16)
        block.sync(f_lds)

        def f_v(v, rep5=rep5):
            for e in range(NCH):
                if e == 0:
                    s_row.wait_at(v, rep5)
                s_dve.inc(v.tensor_sub(stream[:, e, HALO:], stream[:, e, HALO:], m_rep[:]))
                if e >= 1:
                    ae = e - 1
                    s_gp.wait_at(v, g0 + ae + 1)
                    s_io[ae % 2].wait_at(v, io0[ae % 2] + 16 * (ae // 2 + 1))
                    s_dve.inc(v.tensor_add(stream[:, ae, HALO:], stream[:, ae, HALO:],
                                           x15[ae % 2][:]))
            s_gp.wait_at(v, g0 + 8)
            s_io[1].wait_at(v, io0[1] + 64)
            s_dve.inc(v.tensor_add(stream[:, 7, HALO:], stream[:, 7, HALO:], x15[1][:]))
        block.vector(f_v)
        assert s_dve.n == d0 + 16

        def f_g(g):
            for e in range(NCH):
                s_dve.wait_at(g, sub_cnt(e))
                s_gp.inc(g.tensor_mul(stream[:, e, HALO:], stream[:, e, HALO:], r_rep[:]))
        block.gpsimd(f_g)
        assert s_gp.n == g0 + 8

        def f_outs(sync):
            for e in range(NCH):
                s_dve.wait_at(sync, add_cnt(e))
                s_st.inc(sync.dma_start(
                    out=out_x2.rearrange("(c p) t -> p c t", p=128)[:, e, :],
                    in_=stream[:, e, HALO:]), 16)
        block.sync(f_outs)

        def f_fin(sync):
            s_st.wait(sync)
            s_cci.wait(sync)
        block.sync(f_fin)
    ln5()

    _escope("ln5")
    nc._dbg = dict(stream=stream, states=states, small=small, pk=pk, lsm=lsm,
                   qrot=qrot, krot=krot, vtm=vtm, ptil=ptil, o_b=o_b, gbt=gbt,
                   cbw=cbw, recv9=recv9, bfpool=bfpool, f2b=f2b, rnq=rnq, rnk=rnk)
    ctx.close()
    return nc


def _host_prep(inputs, core):
    b, half = core // 2, core % 2
    off = half * T
    x = np.asarray(inputs["L0_attn_x"])
    xm = np.zeros((D, 1 + T), np.float32)
    xm[:, 1:] = x[b, off:off + T].T
    if half == 1:
        xm[:, 0] = x[b, off - 1]
    iot = (np.arange(T, dtype=np.float32) + 1.0).reshape(128, 16)

    n_full = 4096
    DH = 64
    half_idx = np.arange(0, DH, 2, dtype=np.float64)
    inv_freq = 1.0 / 10000.0 ** (half_idx / DH)
    tpos = np.arange(off, off + T, dtype=np.float64)
    freqs = tpos[:, None] * inv_freq[None, :]
    freqs = np.concatenate([freqs, freqs], 1)
    base = (half_idx + 0.4 * DH) / (1.4 * DH)
    power = (tpos - n_full // 2) / 256.0
    scale = base[None, :] ** power[:, None]
    scale = np.concatenate([scale, scale], 1)
    cosv = np.cos(freqs)
    sinv = np.sin(freqs)
    sgn = np.concatenate([-np.ones(32), np.ones(32)])[None, :]
    cq = (cosv * scale).T.astype(np.float32)
    sq = (sinv * scale * sgn).T.astype(np.float32)
    ck = (cosv / scale).T.astype(np.float32)
    sk = (sinv / scale * sgn).T.astype(np.float32)
    cstab = np.stack([np.tile(cq, (2, 1)), np.tile(sq, (2, 1)),
                      np.tile(ck, (2, 1)), np.tile(sk, (2, 1))])

    jj = np.arange(128)[:, None]
    ii = np.arange(128)[None, :]
    utmask = np.where(jj <= ii, 0.0, -1e30).astype(ml_dtypes.bfloat16)
    ltmask = np.where(jj >= ii, 0.0, -1e30).astype(ml_dtypes.bfloat16)
    idmat = np.eye(128).astype(ml_dtypes.bfloat16)

    qs = np.asarray(inputs["L0_ffn_q_scale"], np.float32)
    ks = np.asarray(inputs["L0_ffn_k_scale"], np.float32)
    qkpat = np.zeros((2, 8, 128, 128), np.float32)
    for l in range(2):
        for cc in range(8):
            sv = qs[l] if cc < 4 else ks[l]
            for mh in range(2):
                krow = 32 * (cc % 4) + mh
                qkpat[l, cc, krow, mh * 64:(mh + 1) * 64] = sv

    b1 = np.asarray(inputs["L0_ffn_b1"], np.float32)
    b1pad = np.zeros((2, 44, 128), np.float32)
    for l in range(2):
        for ki, (h0, hn) in enumerate(HKS):
            b1pad[l, ki, :hn] = b1[l, h0:h0 + hn]
            b1pad[l, 22 + ki, :hn] = b1[l, HID + h0:HID + h0 + hn]
    b1p = b1pad.transpose(0, 2, 1)

    im = {
        "x_fm": np.ascontiguousarray(xm),
        "WinT": np.ascontiguousarray(np.asarray(inputs["L0_attn_W_in"], np.float32).T),
        "WoutT": np.ascontiguousarray(np.asarray(inputs["L0_attn_W_out"], np.float32).T),
        "b_inp": np.ascontiguousarray(np.asarray(inputs["L0_attn_b_in"], np.float32).reshape(NCH, 128).T),
        "b_outp": np.ascontiguousarray(np.asarray(inputs["L0_attn_b_out"], np.float32).reshape(NCH, 128).T),
        "scal": np.array([[float(inputs["L0_attn_decay"]), float(inputs["L0_attn_mix"]), 0, 0, 0, 0, 0, 0]], np.float32),
        "iotapk": np.ascontiguousarray(iot),
        "rolef": np.full((128, 1), float(half), np.float32),
        "rbias": np.full((128, 1), 0.0 if half == 1 else NEG, np.float32),
        "sel": np.array([[(core - 1) % 8]], np.int32),
        "wqkv": np.ascontiguousarray(np.asarray(inputs["L0_ffn_wqkv"], np.float32)),
        "wo_i": np.ascontiguousarray(np.asarray(inputs["L0_ffn_wo"], np.float32)),
        "w1_i": np.ascontiguousarray(np.asarray(inputs["L0_ffn_w1"], np.float32)),
        "w2_i": np.ascontiguousarray(np.asarray(inputs["L0_ffn_w2"], np.float32)),
        "b1p": np.ascontiguousarray(b1p),
        "b2p": np.ascontiguousarray(np.stack([np.asarray(inputs["L0_ffn_b2"], np.float32)[l].reshape(NCH, 128).T for l in range(2)])),
        "lgp": np.ascontiguousarray(np.stack([np.asarray(inputs["L0_ffn_ln_g"], np.float32)[l].reshape(NCH, 128).T for l in range(2)])),
        "lbp": np.ascontiguousarray(np.stack([np.asarray(inputs["L0_ffn_ln_b"], np.float32)[l].reshape(NCH, 128).T for l in range(2)])),
        "qsv": np.ascontiguousarray(qs[:, None, :]),
        "ksv": np.ascontiguousarray(ks[:, None, :]),
        "qkpat": np.ascontiguousarray(qkpat),
        "cstab4": np.ascontiguousarray(cstab.astype(np.float32)),
        "idmi": np.ascontiguousarray(idmat),
        "utm": np.ascontiguousarray(utmask),
        "ltm": np.ascontiguousarray(ltmask),
    }
    return im


def kernel(**inputs):
    if "nc" not in _cache:
        _cache["nc"] = build_nc()
    nc = _cache["nc"]
    in_maps = [_host_prep(inputs, c) for c in range(8)]
    res = run_bass_kernel_spmd(nc, in_maps, core_ids=list(range(8)))
    out = np.zeros((4, 4096, D), np.float32)
    for c in range(8):
        b, half = c // 2, c % 2
        out[b, half * T:(half + 1) * T] = res.results[c]["out_x2"].T
    return out


# version 13
# speedup vs baseline: 1.0088x; 1.0088x over previous
from contextlib import ExitStack
import numpy as np
import ml_dtypes
import concourse.bass as bass
import concourse.mybir as mybir
from concourse.bass_utils import run_bass_kernel_spmd

F32 = mybir.dt.float32
BF16 = mybir.dt.bfloat16
I32 = mybir.dt.int32
AF = mybir.ActivationFunctionType
AL = mybir.AluOpType

D = 1024
NCH = 8
T = 2048
TT = 512
NTT = 4
HALO = 1
RG = [[0, 1, 2, 3, 4, 5, 6, 7]]
EPS = 1e-5
HID = 2730
NEG = -3.0e38
KVW = 4 * 512 + 4 * 520
HKS = [(i * 128, 128) for i in range(21)] + [(2688, 42)]

_cache = {}
DBG_STOP = None


class Sem:
    def __init__(self, ctx, nc, name):
        self.h = ctx.enter_context(nc.semaphore(name))
        self.n = 0

    def inc(self, inst, k=1):
        inst.then_inc(self.h, k)
        self.n += k
        return inst

    def wait(self, eng):
        eng.wait_ge(self.h, self.n)

    def wait_at(self, eng, val):
        eng.wait_ge(self.h, val)


def ceil_div(a, b):
    return (a + b - 1) // b


def build_nc():
    nc = bass.Bass("TRN2", target_bir_lowering=False, debug=False,
                   detect_race_conditions=True)
    nc.container_num_cores = 8
    ctx = ExitStack()

    def din(name, shape, dtype=F32):
        return nc.dram_tensor(name, shape, dtype, kind="ExternalInput").ap()

    x_fm = din("x_fm", [D, 1 + T])
    WinT = din("WinT", [D, D])
    WoutT = din("WoutT", [D, D])
    b_inp = din("b_inp", [128, NCH])
    b_outp = din("b_outp", [128, NCH])
    scal = din("scal", [1, 8])
    iotapk = din("iotapk", [128, 16])
    rolef = din("rolef", [128, 1])
    rbias = din("rbias", [128, 1])
    sel = din("sel", [1, 1], I32)
    wqkv = din("wqkv", [2, D, 3 * 512])
    wo_i = din("wo_i", [2, 512, D])
    w1_i = din("w1_i", [2, D, 5460])
    w2_i = din("w2_i", [2, HID, D])
    b1p = din("b1p", [2, 128, 44])
    b2p = din("b2p", [2, 128, NCH])
    lgp = din("lgp", [2, 128, NCH])
    lbp = din("lbp", [2, 128, NCH])
    qsv = din("qsv", [2, 1, 64])
    ksv = din("ksv", [2, 1, 64])
    qkpat = din("qkpat", [2, 8, 128, 128])
    cstab4 = din("cstab4", [4, 128, T])
    idmi = din("idmi", [128, 128], BF16)
    utm = din("utm", [128, 128], BF16)
    ltm = din("ltm", [128, 128], BF16)

    out_x2 = nc.dram_tensor("out_x2", [D, T], F32, kind="ExternalOutput").ap()

    att_sc = nc.dram_tensor("att_sc", [D, T], F32, kind="Internal").ap()
    x1_sc = nc.dram_tensor("x1_sc", [D, T], F32, kind="Internal").ap()
    rowb = nc.dram_tensor("rowb", [4, T], F32, kind="Internal").ap()
    rowb_flat = rowb.rearrange("a t -> (a t)")

    ag_in, ag_out = {}, {}

    def agbuf(name, n, dtype=F32):
        ag_in[name] = nc.dram_tensor(f"agi_{name}", [1, n], dtype, kind="Internal").ap()
        ag_out[name] = nc.dram_tensor(f"ago_{name}", [8, n], dtype, kind="Internal").ap()

    agbuf("e0", 128 * 9)
    agbuf("e1", 128 * 8, BF16)
    for l in range(2):
        agbuf(f"e2a_{l}", 128 * 8)
        agbuf(f"e2b_{l}", 128 * KVW, BF16)
        agbuf(f"e3_{l}", 128 * 8)

    sb = ctx.enter_context
    stream = sb(nc.sbuf_tensor("stream", [128, NCH, 1 + T], F32))
    statm = sb(nc.sbuf_tensor("statm", [1, 2 * TT], F32))
    pk = sb(nc.sbuf_tensor("pk", [128, 160], F32))
    small = sb(nc.sbuf_tensor("small", [128, 160], F32))
    smallb = sb(nc.sbuf_tensor("smallb", [128, 32], BF16))
    onesb = sb(nc.sbuf_tensor("onesb", [128, 8], BF16))
    scratch = sb(nc.sbuf_tensor("scratch", [128, 32768], F32))
    recv9 = sb(nc.sbuf_tensor("recv9", [128, 16], F32))
    ssel = sb(nc.sbuf_tensor("ssel", [1, 2], I32))
    rowr = sb(nc.sbuf_tensor("rowr", [65, TT], F32))
    rowrb = sb(nc.sbuf_tensor("rowrb", [65, TT], BF16))
    ones64 = sb(nc.sbuf_tensor("ones64", [65, 64], BF16))
    qkp = sb(nc.sbuf_tensor("qkp", [128, 8, 128], BF16))
    lsm = sb(nc.sbuf_tensor("lsm", [128, 224], F32))
    idm = sb(nc.sbuf_tensor("idm", [128, 128], BF16))
    maskut = sb(nc.sbuf_tensor("maskut", [128, 128], BF16))
    masklt = sb(nc.sbuf_tensor("masklt", [128, 128], BF16))
    bdon = sb(nc.sbuf_tensor("bdon", [128, 2], BF16))
    ckeep = sb(nc.sbuf_tensor("ckeep", [128, NCH], F32))

    def pkc(i, n=1):
        return pk[:, 16 * i:16 * (i + 1 + (n - 1))]

    SL = dict(bin=0, bout=8, scal=16, rolef=24, beta=25, om=26, dlog=27, eps=29,
              rbias=30, zkeep=32, zhal=40, sinb=48, mlast=56)

    def sm(name, n=1, off=0):
        o = SL[name] + off
        return small[:, o:o + n]

    def carve_bf(off, shape):
        flat = int(np.prod(shape[1:]))
        v = scratch[:, off:off + ceil_div(flat, 2)].bitcast(BF16)
        if len(shape) == 3:
            v = v.rearrange("p (a b) -> p a b", a=shape[1])
        return v

    winb = carve_bf(0, [128, NCH, D])
    woutb = carve_bf(4096, [128, NCH, D])
    z2 = [scratch[:, 8192 + i * 2049: 8192 + (i + 1) * 2049] for i in range(2)]
    h2 = [scratch[:, 8192 + i * T: 8192 + (i + 1) * T] for i in range(2)]
    xb_t = carve_bf(12544, [128, 1, TT]).rearrange("p 1 t -> p t")
    sq_t = carve_bf(12800, [128, 1, TT]).rearrange("p 1 t -> p t")
    mcomb_b = carve_bf(13056, [128, 1, T]).rearrange("p 1 t -> p t")
    dp_rep = carve_bf(13056, [128, 1, T]).rearrange("p 1 t -> p t")
    states = carve_bf(14080, [128, NCH, T])
    bfpool = carve_bf(22784, [128, NCH, 1 + T])
    wincs_b = carve_bf(30980, [128, 1, D]).rearrange("p 1 t -> p t")
    xb_t2 = carve_bf(22784, [128, 1, TT]).rearrange("p 1 t -> p t")
    sq_t2 = carve_bf(23040, [128, 1, TT]).rearrange("p 1 t -> p t")
    xb2b = carve_bf(5120, [128, 1, TT]).rearrange("p 1 t -> p t")
    sq2b = carve_bf(5376, [128, 1, TT]).rearrange("p 1 t -> p t")
    attf = [scratch[:, i * TT:(i + 1) * TT] for i in range(2)]
    r_rep = scratch[:, 1024:1024 + T]
    m_rep = scratch[:, 3072:3072 + T]
    t1_t = scratch[:, 5120:5120 + TT]
    mixr_rep = scratch[:, 14080:14080 + T]

    gen2 = attf
    cstab3 = scratch[:, 5632:5632 + 2048].rearrange("p (a b) -> p a b", a=4)
    qkpf_st = scratch[:, 5632:5632 + 1024]
    xb2 = carve_bf(7680, [128, 1, TT]).rearrange("p 1 t -> p t")
    sq2 = carve_bf(7936, [128, 1, TT]).rearrange("p 1 t -> p t")
    rnkb = carve_bf(7936, [128, 1, TT]).rearrange("p 1 t -> p t")
    wqkvb = carve_bf(8192, [128, NCH, 1536])
    wob = carve_bf(14336, [128, 4, 1024])
    qrot = carve_bf(16384, [128, 4, 512])
    qrot0 = carve_bf(1024, [128, 4, 512])
    rcpb = scratch[:, 2048:2560]
    krot = carve_bf(17408, [128, 12, 512])
    vtm = carve_bf(20480, [128, 12, 520])
    khalo = carve_bf(23600, [128, 4, 512])
    vhalo = carve_bf(24624, [128, 4, 520])
    ptil = carve_bf(25664, [128, 8, 512])
    rotst = carve_bf(27712, [128, 8, 512])
    o_b = rotst
    cbw = carve_bf(29760, [128, NCH, 513])
    rnq = t1_t
    rnk = scratch[:, 31812:31812 + TT]
    rnqb = carve_bf(32324, [128, 1, TT]).rearrange("p 1 t -> p t")
    f2b = carve_bf(8192, [128, 11, 1024])
    w2t = carve_bf(16384, [128, 11, 1024])
    w1t = [carve_bf(23600 + i * 1024, [128, NCH, 256]) for i in range(2)]
    gbt = carve_bf(25664, [128, NCH, 1024])

    pbk = [ctx.enter_context(nc.psum_tensor(f"pb{i}", [128, 512], F32)) for i in range(8)]

    s_ldx = Sem(ctx, nc, "ldx")
    s_ldw = Sem(ctx, nc, "ldw")
    s_mm = Sem(ctx, nc, "mm")
    s_act = Sem(ctx, nc, "act")
    s_dve = Sem(ctx, nc, "dve")
    s_gp = Sem(ctx, nc, "gp")
    s_cc = Sem(ctx, nc, "cc")
    s_cci = Sem(ctx, nc, "cci")
    s_row = Sem(ctx, nc, "srow")
    s_st = Sem(ctx, nc, "st")
    s_io = [Sem(ctx, nc, "io0"), Sem(ctx, nc, "io1")]
    s_w2 = Sem(ctx, nc, "w2l")
    s_w1 = [Sem(ctx, nc, "w1a"), Sem(ctx, nc, "w1b")]
    w1_hist = []
    w1_g = [0]
    f2_cnt = [0, 0]
    gel_s = [0, 0, 0]
    w1am_s = [0, 0]
    f2_s = [0, 0, 0]

    block = ctx.enter_context(nc.Block())

    class _BlockProxy:
        def __init__(self, blk, stopped):
            self._blk = blk
            self._stopped = stopped

        def __getattr__(self, name):
            f = getattr(self._blk, name)

            def wrap(fn, *a, **k):
                if self._stopped[0]:
                    return None
                return f(fn, *a, **k)
            return wrap

    _stopflag = [False]
    block = _BlockProxy(block, _stopflag)

    def _ck(tag):
        if DBG_STOP is not None and DBG_STOP == tag:
            _stopflag[0] = True

    _scopes = {}

    def _scope(name):
        _scopes[name] = nc.enter_named_scope(name, False)[0]

    def _escope(name):
        nc.leave_named_scope(name, _scopes[name], False)
    tok = lambda t0, n: slice(HALO + t0, HALO + t0 + n)

    _scope("p01_loads")
    def p0_sync(sync):
        s_ldx.inc(sync.dma_start(out=stream[:, :, :], in_=x_fm.rearrange("(c p) t -> p c t", p=128)), 16)
        s_ldx.inc(sync.dma_start(out=small[:, 0:8], in_=b_inp[:]), 16)
        s_ldx.inc(sync.dma_start(out=small[:, 8:16], in_=b_outp[:]), 16)
        s_ldx.inc(sync.dma_start(out=small[:, 16:24], in_=scal.partition_broadcast(128).rearrange("p 1 x -> p x")), 16)
        s_ldx.inc(sync.dma_start(out=small[:, 24:25], in_=rolef[:]), 16)
        s_ldx.inc(sync.dma_start(out=small[:, 30:31], in_=rbias[:]), 16)
        s_ldx.inc(sync.dma_start(out=pk[:, 144:160], in_=iotapk[:]), 16)
        s_ldx.inc(sync.dma_start(out=ssel[0:1, 0:1], in_=sel[:]), 16)
        s_ldx.inc(sync.dma_start(out=idm[:], in_=idmi[:]), 16)
        s_ldx.inc(sync.dma_start(out=maskut[:], in_=utm[:]), 16)
        s_ldx.inc(sync.dma_start(out=masklt[:], in_=ltm[:]), 16)
    block.sync(p0_sync)
    LD_ALL = s_ldx.n

    def p0_gp(gp):
        s_ldw.inc(gp.dma_start(out=winb[:], in_=WinT.rearrange("(c p) e -> p c e", p=128)), 16)
        s_ldw.inc(gp.dma_start(out=woutb[:], in_=WoutT.rearrange("(c p) e -> p c e", p=128)), 16)
    block.gpsimd(p0_gp)

    def p1_v(v):
        v.memset(onesb[:], 1.0)
        v.memset(small[:, 29:30], float(EPS))
        v.memset(recv9[:], 0.0)
        v.memset(ones64[:], 1.0)
        v.memset(bdon[:], 0.0)
        s_dve.inc(v.memset(smallb[:], 0.0))
        s_dve.wait(v)
        v.memset(bdon[0:64, 0:1], 1.0)
        s_dve.inc(v.memset(bdon[64:128, 1:2], 1.0))
        s_ldx.wait_at(v, LD_ALL)
        v.tensor_scalar(sm("om"), small[:, 17:18], -1.0, 1.0, AL.mult, AL.add)
        s_dve.inc(v.reciprocal(small[:, 28:29], small[:, 17:18]))
        s_dve.wait(v)
        s_dve.inc(v.tensor_mul(sm("beta"), sm("om"), small[:, 28:29]))
    block.vector(p1_v)

    def p1_a(a):
        s_ldx.wait_at(a, LD_ALL)
        s_act.inc(a.activation(sm("dlog"), small[:, 16:17], AF.Ln))
        s_act.wait(a)
        s_act.inc(a.activation(pkc(8), pk[:, 144:160], AF.Exp, scale=sm("dlog")))
    block.scalar(p1_a)

    def ln_stats(src_tile_fn, first_wait_fn, bank, bank2, xbts, sqts):
        pe0 = s_mm.n
        a0 = s_act.n
        d0 = s_dve.n
        r0 = s_row.n

        def f_sqs(a):
            for j in range(4 * NCH):
                tti, e = divmod(j, NCH)
                if j == 0:
                    first_wait_fn(a)
                if j >= 2:
                    s_mm.wait_at(a, pe0 + j - 1)
                s_act.inc(a.activation(sqts[j % 2][:], src_tile_fn(e, tti), AF.Square))
        block.scalar(f_sqs)
        assert s_act.n == a0 + 32

        def f_cst(v):
            for j in range(4 * NCH):
                tti, e = divmod(j, NCH)
                if j == 0:
                    first_wait_fn(v)
                if j >= 2:
                    s_mm.wait_at(v, pe0 + j - 1)
                s_dve.inc(v.tensor_copy(xbts[j % 2][:], src_tile_fn(e, tti)))
                if e == NCH - 1:
                    s_mm.wait_at(v, pe0 + NCH * (tti + 1))
                    if tti >= 1:
                        s_row.wait_at(v, r0 + 32 * tti)
                    s_dve.inc(v.tensor_copy(statm[0:1, 0:TT], bank[0:1, :]))
                    s_dve.inc(v.tensor_copy(statm[0:1, TT:2 * TT], bank2[0:1, :]))
        block.vector(f_cst)
        assert s_dve.n == d0 + 40

        def f_mms(t):
            for j in range(4 * NCH):
                tti, e = divmod(j, NCH)
                s_act.wait_at(t, a0 + j + 1)
                s_dve.wait_at(t, d0 + 10 * tti + e + 1)
                t.matmul(bank[0:1, :], onesb[:, 0:1], xbts[j % 2][:],
                         start=(e == 0), stop=(e == NCH - 1))
                s_mm.inc(t.matmul(bank2[0:1, :], onesb[:, 0:1], sqts[j % 2][:],
                                  start=(e == 0), stop=(e == NCH - 1)))
        block.tensor(f_mms)
        assert s_mm.n == pe0 + 32

        def f_sts(sync):
            for tti in range(NTT):
                if tti == 0:
                    s_row.wait_at(sync, r0)
                s_dve.wait_at(sync, d0 + 10 * tti + 10)
                s_row.inc(sync.dma_start(out=rowb[0:1, tti * TT:(tti + 1) * TT],
                                         in_=statm[0:1, 0:TT]), 16)
                s_row.inc(sync.dma_start(out=rowb[1:2, tti * TT:(tti + 1) * TT],
                                         in_=statm[0:1, TT:2 * TT]), 16)
        block.sync(f_sts)

    def pk_stats():
        def f_g(sync):
            s_row.wait(sync)
            s_row.inc(sync.dma_start(out=pkc(0), in_=rowb[0:1, :].rearrange("1 (p j) -> p j", j=16)), 16)
            s_row.inc(sync.dma_start(out=pkc(1), in_=rowb[1:2, :].rearrange("1 (p j) -> p j", j=16)), 16)
        block.sync(f_g)

        def f_m(v):
            s_row.wait(v)
            v.tensor_scalar_mul(pkc(2), pkc(0), 1.0 / D)
            s_dve.inc(v.tensor_scalar_mul(pkc(3), pkc(1), 1.0 / D))
            s_dve.wait(v)
            s_dve.inc(v.tensor_mul(pkc(4), pkc(2), pkc(2)))
            s_dve.wait(v)
            s_dve.inc(v.tensor_sub(pkc(3), pkc(3), pkc(4)))
        block.vector(f_m)

        def f_a(a):
            s_dve.wait(a)
            s_act.inc(a.activation(pkc(4), pkc(3), AF.Sqrt, bias=sm("eps")))
        block.scalar(f_a)

        def f_m2(v):
            s_act.wait(v)
            s_dve.inc(v.reciprocal(pkc(5), pkc(4)))
            s_dve.wait(v)
            s_dve.inc(v.tensor_mul(pkc(6), pkc(2), pkc(5)))
        block.vector(f_m2)

    def scatter_pk(slot, row):
        def f_s(sync, slot=slot, row=row):
            s_dve.wait(sync)
            s_row.inc(sync.dma_start(out=rowb[row:row + 1, :].rearrange("1 (p j) -> p j", j=16), in_=pkc(slot)), 16)
        block.sync(f_s)

    def replicate(row, dst):
        def f_r(sync, row=row, dst=dst):
            s_row.wait(sync)
            s_row.inc(sync.dma_start(out=dst, in_=rowb[row:row + 1, :].partition_broadcast(128).rearrange("p 1 t -> p t")), 16)
        block.sync(f_r)

    def ag_event(name, stage_ap, recv_ap):
        d_ready = s_dve.n
        g_ready = s_gp.n
        r_ready = s_row.n
        a_ready = s_act.n

        def f_send(g):
            s_dve.wait_at(g, d_ready)
            s_gp.wait_at(g, g_ready)
            s_row.wait_at(g, r_ready)
            s_act.wait_at(g, a_ready)
            k = stage_ap.shape[-1]
            s_cci.inc(g.dma_start(out=ag_in[name][0, :].rearrange("(p c) -> p c", c=k),
                                  in_=stage_ap), 16)
            s_cci.wait(g)
            s_cc.inc(g.collective_compute(
                "AllGather", AL.bypass,
                ins=[ag_in[name][:]],
                outs=[ag_out[name][:]],
                replica_groups=RG,
            ))
        block.gpsimd(f_send)

        def f_recv(sync):
            s_cc.wait(sync)
            slot = sync.value_load(ssel[0:1, 0:1])
            k = recv_ap.shape[-1]
            s_row.inc(sync.dma_start(out=recv_ap,
                                     in_=ag_out[name][bass.ds(slot, 1), :].rearrange("1 (p c) -> p c", c=k)), 16)
        block.sync(f_recv)

    def ln_rep():
        scatter_pk(5, 2)
        scatter_pk(2, 3)

        def f_qg(sync, mprev=s_mm.n):
            s_mm.wait_at(sync, mprev)
        block.sync(f_qg)
        replicate(2, r_rep[:])
        replicate(3, m_rep[:])
        return s_row.n

    _escope("p01_loads")
    _scope("p2_ln1")
    def ln1_first_wait(eng):
        s_ldx.wait_at(eng, LD_ALL)
    ln_stats(lambda e, tti: stream[:, e, tok(tti * TT, TT)], ln1_first_wait,
             pbk[2], pbk[4], (xb_t, xb_t2), (sq_t, sq_t2))
    pk_stats()

    def p2_mixr():
        def f_m(v):
            s_dve.wait(v)
            s_dve.inc(v.tensor_scalar_mul(pkc(7), pkc(5), small[:, 17:18]))
        block.vector(f_m)
        scatter_pk(7, 2)
        replicate(2, mixr_rep[:])
    p2_mixr()

    def p2_mcomb():
        scatter_pk(6, 3)

        def f_s2(sync):
            s_row.wait(sync)
            s_row.inc(sync.dma_start(out=recv9[0:1, 8:9], in_=rowb[3:4, T - 1:T]), 16)
            s_row.inc(sync.dma_start(out=pkc(4), in_=rowb_flat[3 * T - 1:4 * T - 1].rearrange("(p j) -> p j", j=16)), 16)
        block.sync(f_s2)

        def f_m(v):
            s_row.wait(v)
            s_dve.inc(v.tensor_scalar_mul(pkc(9), pkc(6), small[:, 17:18]))
            s_dve.wait(v)
            s_dve.inc(v.scalar_tensor_tensor(pkc(9), pkc(4), sm("om"), pkc(9), AL.mult, AL.add))
        block.vector(f_m)
        scatter_pk(9, 0)

        def f_sc2(g):
            s_row.wait(g)
            s_cci.inc(g.dma_start(out=mcomb_b[0:1, :], in_=rowb[0:1, :]), 16)
        block.gpsimd(f_sc2)
    p2_mcomb()

    def p2_z():
        for e in range(NCH):
            zsl = z2[e % 2]
            gcnt = s_gp.n

            def f_p1(v, e=e, zsl=zsl, gcnt=gcnt):
                s_row.wait(v)
                if e >= 2:
                    s_gp.wait_at(v, gcnt)
                s_dve.inc(v.tensor_mul(zsl[:, 1:1 + T], stream[:, e, tok(0, T)], mixr_rep[:]))
                s_dve.wait(v)
                v.tensor_copy(small[:, 32 + e:33 + e], zsl[:, 1:2])
                s_dve.inc(v.tensor_copy(recv9[:, e:e + 1], zsl[:, 1 + T - 1:1 + T]))
            block.vector(f_p1)
            dcnt_after = s_dve.n

            def f_p2a(v, e=e, zsl=zsl, dcnt=dcnt_after):
                s_dve.wait_at(v, dcnt)
                s_gp.inc(v.scalar_tensor_tensor(bfpool[:, e, HALO + 1:HALO + T],
                                                zsl[:, 1:T], sm("beta"),
                                                zsl[:, 2:1 + T], AL.mult, AL.add))
            block.vector(f_p2a)
    p2_z()

    ag_event("e0", recv9[:, 0:9], recv9[:, 0:9])

    def p2_fix():
        def f_h(v):
            s_row.wait(v)
            s_dve.inc(v.tensor_scalar_mul(small[:, 40:48], recv9[:, 0:8], small[:, 24:25]))
        block.vector(f_h)

        def f_c1(v):
            s_dve.wait(v)
            for e in range(NCH):
                v.scalar_tensor_tensor(bfpool[:, e, HALO:HALO + 1], small[:, 40 + e:41 + e],
                                       sm("beta"), small[:, 32 + e:33 + e], AL.mult, AL.add)
            v.tensor_mul(small[0:1, 56:57], recv9[0:1, 8:9], small[0:1, 24:25])
            s_dve.inc(v.tensor_scalar_mul(small[0:1, 57:58], pk[0:1, 96:97], small[0:1, 17:18]))
            s_dve.wait(v)
            s_dve.inc(v.scalar_tensor_tensor(small[0:1, 58:59], small[0:1, 56:57],
                                             sm("om")[0:1, :], small[0:1, 57:58], AL.mult, AL.add))
        block.vector(f_c1)

        def f_c2(sync):
            s_dve.wait(sync)
            s_cci.wait(sync)
            s_row.inc(sync.dma_start(out=rowb[0:1, 0:1], in_=small[0:1, 58:59]), 16)
        block.sync(f_c2)

        def f_c3(g):
            s_row.wait(g)
            s_cci.inc(g.dma_start(out=mcomb_b[0:1, 0:1], in_=rowb[0:1, 0:1]), 16)
        block.gpsimd(f_c3)
    p2_fix()

    def p2_wincs():
        for ct in range(2):
            acnt = s_act.n

            def f_mm(t, ct=ct, acnt=acnt):
                if ct == 0:
                    s_ldw.wait_at(t, 32)
                else:
                    s_act.wait_at(t, acnt)
                inst = None
                for dch in range(NCH):
                    inst = t.matmul(pbk[3][0:1, :], onesb[:, 0:1],
                                    winb[:, dch, ct * 512:(ct + 1) * 512],
                                    start=(dch == 0), stop=(dch == NCH - 1))
                s_mm.inc(inst)
            block.tensor(f_mm)

            def f_ev(a, ct=ct, mt=s_mm.n):
                s_mm.wait_at(a, mt)
                s_act.inc(a.activation(wincs_b[0:1, ct * 512:(ct + 1) * 512], pbk[3][0:1, :], AF.Copy, scale=-1.0))
            block.scalar(f_ev)
    p2_wincs()

    _escope("p2_ln1")
    _scope("p3_scan")
    inp_d = s_dve.n
    inp_g = s_gp.n
    inp_a = s_act.n
    inp_ci = s_cci.n

    def p3():
        pe0 = s_mm.n
        a0 = s_act.n
        d0 = s_dve.n

        def f_mms(t):
            for e in range(NCH):
                for k, tti in enumerate([1, 2, 3, 0]):
                    i = e * 4 + k
                    bank = pbk[i % 2]
                    if i == 0:
                        s_dve.wait_at(t, inp_d)
                        s_gp.wait_at(t, inp_g)
                        s_act.wait_at(t, inp_a)
                        s_cci.wait_at(t, inp_ci)
                        s_ldw.wait_at(t, 32)
                    if i >= 2:
                        s_act.wait_at(t, a0 + i - 1)
                    for dch in range(NCH):
                        t.matmul(bank[:, :], winb[:, dch, e * 128:(e + 1) * 128],
                                 bfpool[:, dch, 1 + tti * TT: 1 + (tti + 1) * TT],
                                 start=(dch == 0), stop=False)
                    s_mm.inc(t.matmul(bank[:, :], wincs_b[0:1, e * 128:(e + 1) * 128],
                                      mcomb_b[0:1, tti * TT:(tti + 1) * TT],
                                      start=False, stop=True))
        block.tensor(f_mms)
        assert s_mm.n == pe0 + 32

        def f_evs(a):
            for e in range(NCH):
                hsl = h2[e % 2]
                for k, tti in enumerate([1, 2, 3, 0]):
                    i = e * 4 + k
                    s_mm.wait_at(a, pe0 + i + 1)
                    if k == 0 and e >= 2:
                        s_dve.wait_at(a, d0 + 2 * e - 2)
                    s_act.inc(a.activation(hsl[:, tti * TT:(tti + 1) * TT],
                                           pbk[i % 2][:, :], AF.Relu,
                                           bias=small[:, 0 + e:1 + e]))
        block.scalar(f_evs)
        assert s_act.n == a0 + 32

        def f_scans(v):
            for e in range(NCH):
                hsl = h2[e % 2]
                s_act.wait_at(v, a0 + 4 * (e + 1))
                dd0 = small[:, 16:17].broadcast_to([128, T])
                s_dve.inc(v.tensor_tensor_scan(states[:, e, :], dd0, hsl[:, 0:T],
                                               0.0, AL.mult, AL.add))
                s_dve.wait(v)
                s_dve.inc(v.tensor_copy(smallb[:, e:e + 1], states[:, e, T - 1:T]))
        block.vector(f_scans)
        assert s_dve.n == d0 + 16
    p3()
    WIN_MM_DONE = s_mm.n

    ag_event("e1", smallb[:, 0:8], smallb[:, 8:16])

    def p3_dp():
        def f_b(sync):
            s_act.wait_at(sync, 2)
            s_row.inc(sync.dma_start(out=rowb[2:3, :].rearrange("1 (p j) -> p j", j=16), in_=pkc(8)), 16)
        block.sync(f_b)

        def f_b2(g, rt=s_row.n):
            s_row.wait_at(g, rt)
            s_mm.wait_at(g, WIN_MM_DONE)
            s_cci.inc(g.dma_start(out=dp_rep[:], in_=rowb[2:3, :].partition_broadcast(128).rearrange("p 1 t -> p t")), 16)
        block.gpsimd(f_b2)
        ci_t = s_cci.n

        def f_s(v, rt=s_row.n, ci_t=ci_t):
            s_row.wait_at(v, rt)
            s_cci.wait_at(v, ci_t)
            s_dve.inc(v.tensor_scalar_mul(small[:, 48:56], smallb[:, 8:16], small[:, 24:25]))
        block.vector(f_s)

        for e in range(NCH):
            def f_c(v, e=e):
                s_dve.wait(v)
                s_dve.inc(v.scalar_tensor_tensor(states[:, e, :], dp_rep[:], small[:, 48 + e:49 + e],
                                                 states[:, e, :], AL.mult, AL.add))
            block.vector(f_c)
    p3_dp()
    STATES_D = s_dve.n

    _escope("p3_scan")
    _scope("p4_wout")
    def p4():
        pe0 = s_mm.n
        a0 = s_act.n
        d0 = s_dve.n
        r0 = s_row.n
        io0 = (s_io[0].n, s_io[1].n)
        xbs = (xb_t, xb_t2)
        sqs = (sq_t, sq_t2)

        def stats_done(j):
            return pe0 + (2 * j + 3 if j < 31 else 64)

        def f_mms(t):
            def stats(j):
                tti, e = divmod(j, NCH)
                s_act.wait_at(t, a0 + 2 * j + 2)
                s_dve.wait_at(t, d0 + 10 * tti + e + 1)
                t.matmul(pbk[2][0:1, :], onesb[:, 0:1], xbs[j % 2][:],
                         start=(e == 0), stop=(e == NCH - 1))
                s_mm.inc(t.matmul(pbk[4][0:1, :], onesb[:, 0:1], sqs[j % 2][:],
                                  start=(e == 0), stop=(e == NCH - 1)))
            for j in range(32):
                tti, e = divmod(j, NCH)
                if j == 0:
                    s_dve.wait_at(t, STATES_D)
                    s_ldw.wait_at(t, 32)
                if j >= 2:
                    s_act.wait_at(t, a0 + 2 * j - 2)
                inst = None
                for dch in range(NCH):
                    inst = t.matmul(pbk[j % 2][:, :], woutb[:, dch, e * 128:(e + 1) * 128],
                                    states[:, dch, tti * TT:(tti + 1) * TT],
                                    start=(dch == 0), stop=(dch == NCH - 1))
                s_mm.inc(inst)
                if j >= 1:
                    stats(j - 1)
            stats(31)
        block.tensor(f_mms)
        assert s_mm.n == pe0 + 64

        def f_evs(a):
            for j in range(32):
                tti, e = divmod(j, NCH)
                s_mm.wait_at(a, pe0 + (2 * j if j >= 1 else 1))
                if j >= 2:
                    s_io[j % 2].wait_at(a, io0[j % 2] + 16 * (j // 2))
                fsl = attf[j % 2]
                s_act.inc(a.activation(fsl[:], pbk[j % 2][:, :], AF.Identity,
                                       bias=small[:, 8 + e:9 + e]))
                s_act.wait(a)
                s_act.inc(a.activation(sqs[j % 2][:], fsl[:], AF.Square))
        block.scalar(f_evs)
        assert s_act.n == a0 + 64

        def f_casts(v):
            for j in range(32):
                tti = j // NCH
                s_act.wait_at(v, a0 + 2 * j + 1)
                if j >= 2:
                    s_mm.wait_at(v, stats_done(j - 2))
                s_dve.inc(v.tensor_copy(xbs[j % 2][:], attf[j % 2][:]))
                if j % NCH == NCH - 1:
                    s_mm.wait_at(v, stats_done(j))
                    if tti >= 1:
                        s_row.wait_at(v, r0 + 32 * tti)
                    s_dve.inc(v.tensor_copy(statm[0:1, 0:TT], pbk[2][0:1, :]))
                    s_dve.inc(v.tensor_copy(statm[0:1, TT:2 * TT], pbk[4][0:1, :]))
        block.vector(f_casts)
        assert s_dve.n == d0 + 40

        def f_stores(sync):
            for j in range(32):
                tti, e = divmod(j, NCH)
                s_act.wait_at(sync, a0 + 2 * j + 1)
                s_io[j % 2].inc(sync.dma_start(
                    out=att_sc.rearrange("(c p) t -> p c t", p=128)[:, e, tti * TT:(tti + 1) * TT],
                    in_=attf[j % 2][:]), 16)
                if e == NCH - 1:
                    if tti == 0:
                        s_row.wait_at(sync, r0)
                    s_dve.wait_at(sync, d0 + 10 * tti + 10)
                    s_row.inc(sync.dma_start(out=rowb[0:1, tti * TT:(tti + 1) * TT],
                                             in_=statm[0:1, 0:TT]), 16)
                    s_row.inc(sync.dma_start(out=rowb[1:2, tti * TT:(tti + 1) * TT],
                                             in_=statm[0:1, TT:2 * TT]), 16)
        block.sync(f_stores)
    p4()

    pk_stats()
    REP_R = ln_rep()

    def p5():
        wa = [scratch[:, 8192:10240], scratch[:, 10240:12288]]
        d0 = s_dve.n
        g0 = s_gp.n
        io0 = (s_io[0].n, s_io[1].n)

        def sub_cnt(e):
            return d0 + (2 * e if e >= 1 else 1)

        def add_cnt(e):
            return d0 + (2 * e + 3 if e <= 6 else 16)

        def f_lds(sync):
            for e in range(NCH):
                if e == 0:
                    s_io[0].wait_at(sync, io0[0])
                    s_io[1].wait_at(sync, io0[1])
                if e >= 2:
                    s_dve.wait_at(sync, add_cnt(e - 2))
                s_io[e % 2].inc(sync.dma_start(
                    out=wa[e % 2][:],
                    in_=att_sc.rearrange("(c p) t -> p c t", p=128)[:, e, :]), 16)
        block.sync(f_lds)

        def f_v(v):
            for e in range(NCH):
                if e == 0:
                    s_row.wait_at(v, REP_R)
                s_io[e % 2].wait_at(v, io0[e % 2] + 16 * (e // 2 + 1))
                s_dve.inc(v.tensor_sub(wa[e % 2][:], wa[e % 2][:], m_rep[:]))
                if e >= 1:
                    ae = e - 1
                    s_gp.wait_at(v, g0 + ae + 1)
                    s_dve.inc(v.tensor_add(stream[:, ae, HALO:], stream[:, ae, HALO:],
                                           wa[ae % 2][:]))
            s_gp.wait_at(v, g0 + 8)
            s_dve.inc(v.tensor_add(stream[:, 7, HALO:], stream[:, 7, HALO:], wa[1][:]))
        block.vector(f_v)
        assert s_dve.n == d0 + 16

        def f_g(g):
            for e in range(NCH):
                s_dve.wait_at(g, sub_cnt(e))
                s_gp.inc(g.tensor_mul(wa[e % 2][:], wa[e % 2][:], r_rep[:]))
        block.gpsimd(f_g)
        assert s_gp.n == g0 + 8
    p5()

    def p5_store():
        def f_st(sync):
            s_dve.wait(sync)
            s_st.inc(sync.dma_start(out=x1_sc.rearrange("(c p) t -> p c t", p=128), in_=stream[:, :, HALO:]), 16)
        block.sync(f_st)
    p5_store()
    X1_ST = s_st.n

    _escope("p4_wout")
    _scope("p5_ln3")
    def ln3():
        def fw(eng):
            s_dve.wait(eng)
            s_gp.wait(eng)
        ln_stats(lambda e, tti: stream[:, e, tok(tti * TT, TT)], fw, pbk[2], pbk[4], (xb2, xb2b), (sq2, sq2b))
        pk_stats()
        rep3 = ln_rep()

        d0 = s_dve.n
        g0 = s_gp.n

        def f_v3(v, rep3=rep3, x1st=X1_ST):
            for e in range(NCH):
                if e == 0:
                    s_row.wait_at(v, rep3)
                    s_st.wait_at(v, x1st)
                s_dve.inc(v.tensor_sub(stream[:, e, HALO:], stream[:, e, HALO:], m_rep[:]))
        block.vector(f_v3)

        def f_g3(g):
            for e in range(NCH):
                s_dve.wait_at(g, d0 + e + 1)
                s_gp.inc(g.tensor_mul(stream[:, e, HALO:], stream[:, e, HALO:], r_rep[:]))
        block.gpsimd(f_g3)
    ln3()
    _ck("c0")

    _escope("p5_ln3")
    ALIVE = {kc: (max(0, kc - 4), min(3, kc)) for kc in range(8)}

    def kslot(w):
        return 0 if w == 0 else 1 + (w - 1) % 2

    def ffn_layer(l):
        def f_lw(gp, l=l):
            s_mm.wait(gp)
            s_act.wait(gp)
            s_dve.wait(gp)
            s_ldw.inc(gp.dma_start(out=wqkvb[:], in_=wqkv[l].rearrange("(c p) e -> p c e", p=128)), 16)
            s_ldw.inc(gp.dma_start(out=wob[:], in_=wo_i[l].rearrange("(c p) e -> p c e", p=128)), 16)
        block.gpsimd(f_lw)
        LW_T = s_ldw.n

        def f_ls(sync, l=l):
            s_row.wait(sync)
            s_act.wait(sync)
            s_dve.wait(sync)
            s_row.inc(sync.dma_start(out=lsm[:, 0:44], in_=b1p[l]), 16)
            s_row.inc(sync.dma_start(out=lsm[:, 44:52], in_=b2p[l]), 16)
            s_row.inc(sync.dma_start(out=lsm[:, 52:60], in_=lgp[l]), 16)
            s_row.inc(sync.dma_start(out=lsm[:, 60:68], in_=lbp[l]), 16)
            s_row.inc(sync.dma_start(out=lsm[:, 80:144], in_=qsv[l].partition_broadcast(128).rearrange("p 1 x -> p x")), 16)
            s_row.inc(sync.dma_start(out=lsm[:, 144:208], in_=ksv[l].partition_broadcast(128).rearrange("p 1 x -> p x")), 16)
        block.sync(f_ls)

        def f_ls2(sync, l=l, dprev=s_dve.n):
            s_dve.wait_at(sync, dprev)
            s_row.inc(sync.dma_start(out=qkpf_st.rearrange("p (a m) -> p a m", a=8), in_=qkpat[l].rearrange("a p m -> p a m")), 16)
        block.sync(f_ls2)
        LS_T = s_row.n

        def f_cb(v, l=l):
            s_row.wait_at(v, LS_T)
            v.tensor_reduce(lsm[:, 76:77], lsm[:, 80:144], mybir.AxisListType.X, AL.max,
                            apply_absolute_value=True)
            s_dve.inc(v.tensor_reduce(lsm[:, 78:79], lsm[:, 144:208], mybir.AxisListType.X, AL.max,
                                      apply_absolute_value=True))
            s_dve.wait(v)
            s_dve.inc(v.tensor_mul(lsm[:, 76:77], lsm[:, 76:77], lsm[:, 78:79]))
            s_dve.wait(v)
            s_dve.inc(v.tensor_scalar_mul(lsm[:, 76:77], lsm[:, 76:77], -8.0))
            s_dve.wait(v)
            s_dve.inc(v.tensor_add(lsm[:, 77:78], lsm[:, 76:77], sm("rbias")))
            s_dve.inc(v.tensor_copy(qkp[:], qkpf_st.rearrange("p (a m) -> p a m", a=8)))
        block.vector(f_cb)
        CB_T = s_dve.n

        def f_vone(v):
            s_dve.wait(v)
            for sl in range(12):
                v.memset(vtm[:, sl, :].rearrange("p (h c) -> p h c", h=8)[:, :, 64:65], 1.0)
            s_dve.inc(v.memset(vhalo[:, :, :].rearrange("p a (h c) -> p a h c", h=8)[:, :, :, 64:65], 1.0))
        block.vector(f_vone)

        def f_e2a_st(v):
            s_dve.wait(v)
            s_gp.wait(v)
            inst = None
            for e in range(NCH):
                inst = v.tensor_copy(recv9[:, e:e + 1], stream[:, e, HALO + T - 1:HALO + T])
            s_dve.inc(inst)
        block.vector(f_e2a_st)
        ag_event(f"e2a_{l}", recv9[:, 0:8], recv9[:, 0:8])

        def f_e2a_fix(v):
            s_row.wait(v)
            s_dve.inc(v.tensor_scalar_mul(small[:, 40:48], recv9[:, 0:8], small[:, 24:25]))
            s_dve.wait(v)
            inst = None
            for e in range(NCH):
                inst = v.tensor_copy(stream[:, e, 0:1], small[:, 40 + e:41 + e])
            s_dve.inc(inst)
        block.vector(f_e2a_fix)
        HALO_D = s_dve.n

        def keep_col(w):
            def f_k(v, w=w):
                s_dve.wait(v)
                s_gp.wait(v)
                inst = None
                for e in range(NCH):
                    inst = v.tensor_copy(ckeep[:, e:e + 1], stream[:, e, w * TT:w * TT + 1])
                s_dve.inc(inst)
            block.vector(f_k)

        def cast_cb(w):
            def f_c(g, w=w, dh=s_dve.n, mm=s_mm.n):
                s_dve.wait_at(g, dh)
                s_gp.wait(g)
                s_mm.wait_at(g, mm)
                g.tensor_copy(cbw[:, :, 0:1], ckeep[:, :].rearrange("p c -> p c 1") if False else ckeep[:, 0:NCH].rearrange("p (a b) -> p a b", a=NCH))
                s_gp.inc(g.tensor_copy(cbw[:, :, 1:513], stream[:, :, w * TT + 1: w * TT + 513]))
            block.gpsimd(f_c)
            return s_gp.n

        def window_prep(w, cbw_t):
            def f_ct(sync, w=w, dprev=s_dve.n):
                s_dve.wait_at(sync, dprev)
                s_row.inc(sync.dma_start(out=cstab3[:, :, :],
                                         in_=cstab4[:, :, w * TT:(w + 1) * TT].rearrange("a p t -> p a t")), 16)
            block.sync(f_ct)
            CT_T = s_row.n

            def f_nz(v, w=w, mprev=s_mm.n):
                s_mm.wait_at(v, mprev)
                v.memset(pbk[5][:, :], 0.0)
                s_dve.inc(v.memset(pbk[6][:, :], 0.0))
            block.vector(f_nz)
            NZ_T = s_dve.n

            pe0 = s_mm.n
            a0 = s_act.n

            def f_pe1(t, w=w, cbw_t=cbw_t, pe0=pe0, a0=a0, nzt=NZ_T, dprev=s_dve.n):
                def nm(cc):
                    s_act.wait_at(t, a0 + 2 * cc + 2)
                    if cc == 0:
                        s_dve.wait_at(t, nzt)
                    nb = pbk[5] if cc < 4 else pbk[6]
                    base = 32 * (cc % 4)
                    s_mm.inc(t.matmul(nb[base:base + 2, :], bdon[:, :], sq2[:],
                                      start=False, stop=False, skip_group_check=True,
                                      tile_position=(0, base)))
                for cc in range(8):
                    if cc == 0:
                        s_gp.wait_at(t, cbw_t)
                        s_ldw.wait_at(t, LW_T)
                        s_dve.wait_at(t, CB_T)
                        s_dve.wait_at(t, dprev)
                    if cc >= 2:
                        s_act.wait_at(t, a0 + 2 * cc - 2)
                    inst = None
                    for dch in range(NCH):
                        off = 0 if dch >= 4 else 1
                        inst = t.matmul(pbk[cc % 2][:, :], wqkvb[:, dch, cc * 128:(cc + 1) * 128],
                                        cbw[:, dch, off:off + TT],
                                        start=(dch == 0), stop=(dch == NCH - 1))
                    s_mm.inc(inst)
                    if cc >= 1:
                        nm(cc - 1)
                nm(7)
            block.tensor(f_pe1)
            assert s_mm.n == pe0 + 16

            def f_evqs(a, pe0=pe0):
                for cc in range(8):
                    s_mm.wait_at(a, pe0 + (2 * cc + 1 if cc >= 1 else 1))
                    s_act.inc(a.activation(ptil[:, cc, :], pbk[cc % 2][:, :], AF.Copy))
                    s_act.inc(a.activation(sq2[:], pbk[cc % 2][:, :], AF.Square))
            block.scalar(f_evqs)
            assert s_act.n == a0 + 16
            QKEV_T = s_act.n

            def f_nrm(a, nm=pe0 + 16):
                s_mm.wait_at(a, nm)
                s_act.inc(a.activation(rnq[:], pbk[5][:, :], AF.Sqrt))
                s_act.inc(a.activation(rnk[:], pbk[6][:, :], AF.Sqrt))
            block.scalar(f_nrm)

            def f_nrm2(v, at=s_act.n):
                s_act.wait_at(v, at)
                s_dve.inc(v.tensor_scalar_max(rnq[:], rnq[:], 1e-12))
                s_dve.wait(v)
                s_dve.inc(v.reciprocal(rnq[:], rnq[:]))
                s_dve.wait(v)
                s_dve.inc(v.tensor_copy(rnqb[:], rnq[:]))
                s_dve.inc(v.tensor_scalar_max(rnk[:], rnk[:], 1e-12))
                s_dve.wait(v)
                s_dve.inc(v.reciprocal(rnk[:], rnk[:]))
                s_dve.wait(v)
                s_dve.inc(v.tensor_copy(rnkb[:], rnk[:]))
            block.vector(f_nrm2)
            NRS_T = s_act.n

            pe1 = s_mm.n
            d1 = s_dve.n
            QN_D = d1 - 3

            BCB = (pbk[7], pbk[4])

            def f_bcs(t, pe1=pe1, d1=d1, nrs=NRS_T, qn=QN_D):
                for cc in range(8):
                    if cc == 0:
                        s_act.wait_at(t, nrs)
                    if cc < 2:
                        s_dve.wait_at(t, qn)
                    else:
                        s_dve.wait_at(t, d1 + cc - 1)
                    rb = rnqb if cc < 4 else rnkb
                    s_mm.inc(t.matmul(BCB[cc % 2][:, :], qkp[:, cc, :], rb[:], start=True, stop=True))
            block.tensor(f_bcs)
            assert s_mm.n == pe1 + 8

            def f_scs(v, pe1=pe1, qk=QKEV_T):
                for cc in range(8):
                    if cc == 0:
                        s_act.wait_at(v, qk)
                    s_mm.wait_at(v, pe1 + cc + 1)
                    s_dve.inc(v.tensor_mul(ptil[:, cc, :], ptil[:, cc, :], BCB[cc % 2][:, :]))
            block.vector(f_scs)
            assert s_dve.n == d1 + 8
            SC_D = s_dve.n

            def f_rsh(sync, sc=SC_D, d1=d1, wo_done=s_mm.n):
                s_dve.wait_at(sync, d1 + 4)
                s_mm.wait_at(sync, wo_done)
                s_io[0].inc(sync.dma_start(out=rotst[0:32, 0:4, :], in_=ptil[32:64, 0:4, :]), 16)
                s_io[0].inc(sync.dma_start(out=rotst[32:64, 0:4, :], in_=ptil[0:32, 0:4, :]), 16)
                s_io[0].inc(sync.dma_start(out=rotst[64:96, 0:4, :], in_=ptil[96:128, 0:4, :]), 16)
                s_io[0].inc(sync.dma_start(out=rotst[96:128, 0:4, :], in_=ptil[64:96, 0:4, :]), 16)
                s_dve.wait_at(sync, sc)
                s_io[1].inc(sync.dma_start(out=rotst[0:32, 4:8, :], in_=ptil[32:64, 4:8, :]), 16)
                s_io[1].inc(sync.dma_start(out=rotst[32:64, 4:8, :], in_=ptil[0:32, 4:8, :]), 16)
                s_io[1].inc(sync.dma_start(out=rotst[64:96, 4:8, :], in_=ptil[96:128, 4:8, :]), 16)
                s_io[1].inc(sync.dma_start(out=rotst[96:128, 4:8, :], in_=ptil[64:96, 4:8, :]), 16)
            block.sync(f_rsh)
            RSH0 = s_io[0].n
            RSH1 = s_io[1].n

            def f_rcs(v, w=w, rsh0=RSH0, rsh1=RSH1, ct=CT_T):
                for cc in range(8):
                    if cc == 0:
                        s_row.wait_at(v, ct)
                        s_io[0].wait_at(v, rsh0)
                    else:
                        s_dve.wait(v)
                    if cc == 4:
                        s_io[1].wait_at(v, rsh1)
                    else:
                        s_dve.wait(v)
                    ct_c = cstab3[:, 0 if cc < 4 else 2, :]
                    ct_s = cstab3[:, 1 if cc < 4 else 3, :]
                    v.tensor_mul(gen2[0][:], ptil[:, cc, :], ct_c)
                    s_dve.inc(v.tensor_mul(gen2[1][:], rotst[:, cc, :], ct_s))
                    s_dve.wait(v)
                    if cc < 4:
                        dst = (qrot if w else qrot0)[:, cc, :]
                    else:
                        dst = krot[:, kslot(w) * 4 + (cc - 4), :]
                    s_dve.inc(v.tensor_add(dst, gen2[0][:], gen2[1][:]))
            block.vector(f_rcs)
            ROT_D = s_dve.n

            vslot = kslot(w)
            pe2 = s_mm.n
            a2 = s_act.n

            def f_vms(t, cbw_t=cbw_t, pe2=pe2, a2=a2):
                for tb in range(4):
                    if tb == 0:
                        s_gp.wait_at(t, cbw_t)
                    if tb >= 2:
                        s_act.wait_at(t, a2 + tb - 1)
                    inst = None
                    for dch in range(NCH):
                        off = (0 if dch >= 4 else 1) + tb * 128
                        inst = t.matmul(pbk[2 + (tb % 2)][:, :], cbw[:, dch, off:off + 128],
                                        wqkvb[:, dch, 1024:1536],
                                        start=(dch == 0), stop=(dch == NCH - 1))
                    s_mm.inc(inst)
            block.tensor(f_vms)
            assert s_mm.n == pe2 + 4

            def f_ves(a, vslot=vslot, pe2=pe2):
                for tb in range(4):
                    s_mm.wait_at(a, pe2 + tb + 1)
                    dst = vtm[:, vslot * 4 + tb, :].rearrange("p (h c) -> p h c", h=8)[:, :, 0:64]
                    s_act.inc(a.activation(dst, pbk[2 + (tb % 2)][:, :], AF.Copy))
            block.scalar(f_ves)
            assert s_act.n == a2 + 4
            return ROT_D, s_act.n

        def attention(w, use_halo, prep_d, prep_a):
            def f_gate(t, dprev=s_dve.n, prep_a=prep_a):
                s_dve.wait_at(t, dprev)
                s_act.wait_at(t, prep_a)
            block.tensor(f_gate)
            qsrc = qrot0 if use_halo else qrot

            def ksrc_of(h, kc):
                hrow = (h % 2) * 64
                hcc = h // 2
                if kc < 4:
                    if use_halo:
                        return khalo[hrow:hrow + 64, hcc, kc * 128:(kc + 1) * 128]
                    return krot[hrow:hrow + 64, kslot(w - 1) * 4 + hcc, kc * 128:(kc + 1) * 128]
                return krot[hrow:hrow + 64, kslot(w) * 4 + hcc, (kc - 4) * 128:(kc - 3) * 128]

            def vb_of(h, kc, av_m):
                if kc < 4:
                    if use_halo:
                        return vhalo[:, kc, h * 65:h * 65 + av_m]
                    return vtm[:, kslot(w - 1) * 4 + kc, h * 65:h * 65 + av_m]
                return vtm[:, kslot(w) * 4 + (kc - 4), h * 65:h * 65 + av_m]

            def emit_oev(v, hp, rb_cnt):
                s_mm.wait_at(v, rb_cnt)
                s_dve.inc(v.tensor_copy(rcpb[:, :], pbk[7][:, :]))
                s_dve.wait(v)
                v.tensor_mul(o_b[0:64, hp, :], pbk[2][0:64, :], rcpb[0:64, :])
                s_dve.inc(v.tensor_mul(o_b[64:128, hp, :], pbk[3][64:128, :], rcpb[64:128, :]))

            cast_end = [0] * 4
            rb_cnt = [0] * 4
            for hp in range(4):
                he, ho = 2 * hp, 2 * hp + 1
                mm0 = s_mm.n
                a0 = s_act.n
                d0 = s_dve.n
                rbofs = 1 if hp >= 1 else 0
                av_end = mm0 + 40 + rbofs
                if hp >= 1:
                    rb_cnt[hp - 1] = mm0 + 9
                HCFG = ((he, 0, 64, pbk[2], 64, 0), (ho, 64, 64, pbk[3], 0, 1))

                def f_dveA(v, hp=hp, mm0=mm0, d0=d0, prep_a=prep_a, av_end=av_end):
                    if hp >= 1:
                        emit_oev(v, hp - 1, mm0 + 9)
                        s_dve.wait(v)
                    s_mm.wait_at(v, mm0)
                    if hp == 0:
                        s_act.wait_at(v, prep_a)
                        s_dve.wait_at(v, d0)
                    v.memset(pbk[2][0:64, :], 0.0)
                    v.memset(pbk[3][64:128, :], 0.0)
                    v.memset(pbk[5][0:1, :], 0.0)
                    s_dve.inc(v.memset(pbk[5][64:65, :], 0.0))
                    s_mm.wait_at(v, av_end)
                    s_dve.inc(v.reciprocal(rowr[64:65, :], pbk[5][64:65, :]))
                    s_dve.inc(v.reciprocal(rowr[0:1, :], pbk[5][0:1, :]))
                    s_dve.wait(v)
                    v.tensor_copy(rowrb[64:65, :], rowr[64:65, :])
                    s_dve.inc(v.tensor_copy(rowrb[0:1, :], rowr[0:1, :]))
                block.vector(f_dveA)
                assert s_dve.n == d0 + 4 + (2 if hp >= 1 else 0)
                cast_end[hp] = s_dve.n
                zav_cnt = d0 + (2 if hp >= 1 else 0) + 1
                rbofs = 1 if hp >= 1 else 0

                def f_pe(t, hp=hp, mm0=mm0, a0=a0, zav=zav_cnt, HCFG=HCFG, rbofs=rbofs,
                         ce=(cast_end[hp - 1] if hp >= 1 else 0)):
                    for h, av_base, av_m, avbank, sumrow, eo in HCFG:
                        hrow = (h % 2) * 64
                        hcc = h // 2
                        for kc in range(8):
                            qb0, qb1 = ALIVE[kc]
                            nq = (qb1 - qb0 + 1) * 128
                            q0 = qb0 * 128
                            s_act.wait_at(t, a0 + eo * 8 + kc - 1)
                            if kc >= 4:
                                mtile = (maskut, (kc - 4) * 128)
                            else:
                                mtile = (masklt, kc * 128)
                            t.matmul(pbk[kc % 2][0:128, q0:q0 + nq], ksrc_of(h, kc),
                                     qsrc[hrow:hrow + 64, hcc, q0:q0 + nq],
                                     start=True, stop=False)
                            s_mm.inc(t.matmul(pbk[kc % 2][0:128, mtile[1]:mtile[1] + 128],
                                              idm[:], mtile[0][:],
                                              start=False, stop=True))
                        if eo == 0 and hp >= 1:
                            s_dve.wait_at(t, ce)
                            t.matmul(pbk[7][0:64, :], ones64[64:65, 0:64], rowrb[64:65, :],
                                     start=True, stop=True)
                            s_mm.inc(t.matmul(pbk[7][64:128, :], ones64[0:1, 0:64], rowrb[0:1, :],
                                              start=True, stop=False, skip_group_check=True,
                                              tile_position=(0, 64)))
                        for kc in range(8):
                            qb0, qb1 = ALIVE[kc]
                            nq = (qb1 - qb0 + 1) * 128
                            q0 = qb0 * 128
                            if eo == 0 and kc == 0:
                                s_dve.wait_at(t, zav)
                            s_act.wait_at(t, a0 + eo * 8 + kc + 1)
                            s_mm.inc(t.matmul(pbk[5][sumrow:sumrow + 1, q0:q0 + nq],
                                              onesb[:, 0:1], ptil[:, kc, q0:q0 + nq],
                                              start=False, stop=False, skip_group_check=True,
                                              tile_position=(0, sumrow)))
                        for kc in range(8):
                            qb0, qb1 = ALIVE[kc]
                            nq = (qb1 - qb0 + 1) * 128
                            q0 = qb0 * 128
                            s_mm.inc(t.matmul(avbank[av_base:av_base + av_m, q0:q0 + nq],
                                              vb_of(h, kc, av_m), ptil[:, kc, q0:q0 + nq],
                                              start=False, stop=False, skip_group_check=True))
                block.tensor(f_pe)
                assert s_mm.n == mm0 + 48 + (1 if hp >= 1 else 0)

                def f_exps(a, mm0=mm0, HCFG=HCFG, rbofs=rbofs):
                    for h, av_base, av_m, avbank, sumrow, eo in HCFG:
                        for kc in range(8):
                            qb0, qb1 = ALIVE[kc]
                            nq = (qb1 - qb0 + 1) * 128
                            q0 = qb0 * 128
                            if eo == 0:
                                s_mm.wait_at(a, mm0 + kc + 1)
                            else:
                                s_mm.wait_at(a, mm0 + rbofs + 24 + kc + 1)
                            bias_ap = lsm[:, 77:78] if (kc < 4 and use_halo) else lsm[:, 76:77]
                            s_act.inc(a.activation(ptil[:, kc, q0:q0 + nq],
                                                   pbk[kc % 2][0:128, q0:q0 + nq],
                                                   AF.Exp, scale=8.0, bias=bias_ap))
                block.scalar(f_exps)
                assert s_act.n == a0 + 16

            def f_rb3(t, ce=cast_end[3]):
                s_dve.wait_at(t, ce)
                t.matmul(pbk[7][0:64, :], ones64[64:65, 0:64], rowrb[64:65, :],
                         start=True, stop=True)
                s_mm.inc(t.matmul(pbk[7][64:128, :], ones64[0:1, 0:64], rowrb[0:1, :],
                                  start=True, stop=False, skip_group_check=True,
                                  tile_position=(0, 64)))
            block.tensor(f_rb3)
            RB3 = s_mm.n

            def f_dveZ(v, rb3=RB3):
                emit_oev(v, 3, rb3)
            block.vector(f_dveZ)
            return s_dve.n

        def wo_c1(w, o_done):
            pe0 = s_mm.n
            d0 = s_dve.n

            def f_wms(t, o_done=o_done, pe0=pe0, d0=d0):
                for e in range(NCH):
                    if e == 0:
                        s_dve.wait_at(t, o_done)
                    if e >= 2:
                        s_dve.wait_at(t, d0 + e - 1)
                    inst = None
                    for hp in range(4):
                        inst = t.matmul(pbk[e % 2][:, :], wob[:, hp, e * 128:(e + 1) * 128],
                                        o_b[:, hp, :],
                                        start=(hp == 0), stop=(hp == 3))
                    s_mm.inc(inst)
            block.tensor(f_wms)
            assert s_mm.n == pe0 + 8

            def f_c1s(v, w=w, pe0=pe0):
                for e in range(NCH):
                    s_mm.wait_at(v, pe0 + e + 1)
                    s_dve.inc(v.tensor_add(stream[:, e, tok(w * TT, TT)],
                                           stream[:, e, tok(w * TT, TT)], pbk[e % 2][:, :]))
            block.vector(f_c1s)
            assert s_dve.n == d0 + 8
            return s_dve.n

        _scope(f"L{l}_win01")
        keep_col(0)
        cb_t = cast_cb(0)
        pd, pa = window_prep(0, cb_t)
        _ck((l, "prep", 0))
        keep_col(1)
        cb_t = cast_cb(1)
        pd, pa = window_prep(1, cb_t)
        _ck((l, "prep", 1))
        _escope(f"L{l}_win01")
        _scope(f"L{l}_win23")
        keep_col(2)
        cb_t = cast_cb(2)
        od = attention(1, False, pd, pa)
        _ck((l, "attn", 1))
        wo_c1(1, od)
        pd, pa = window_prep(2, cb_t)
        keep_col(3)
        cb_t = cast_cb(3)
        od = attention(2, False, pd, pa)
        _ck((l, "attn", 2))
        wo_c1(2, od)
        pd, pa = window_prep(3, cb_t)
        sl3 = kslot(3)

        def f_e2b(g, sl3=sl3, l=l):
            s_dve.wait(g)
            s_gp.wait(g)
            s_act.wait(g)
            s_cci.inc(g.dma_start(out=ag_in[f"e2b_{l}"][0, :].rearrange("(p c) -> p c", c=KVW)[:, 0:4 * 512],
                                  in_=krot[:, sl3 * 4:(sl3 + 1) * 4, :].rearrange("p a t -> p (a t)")), 16)
            s_cci.inc(g.dma_start(out=ag_in[f"e2b_{l}"][0, :].rearrange("(p c) -> p c", c=KVW)[:, 4 * 512:],
                                  in_=vtm[:, sl3 * 4:(sl3 + 1) * 4, :].rearrange("p a t -> p (a t)")), 16)
            s_cci.wait(g)
            s_cc.inc(g.collective_compute(
                "AllGather", AL.bypass,
                ins=[ag_in[f"e2b_{l}"][:]],
                outs=[ag_out[f"e2b_{l}"][:]],
                replica_groups=RG,
            ))
        block.gpsimd(f_e2b)

        def f_e2br(sync, l=l):
            s_cc.wait(sync)
            slot = sync.value_load(ssel[0:1, 0:1])
            src = ag_out[f"e2b_{l}"][bass.ds(slot, 1), :].rearrange("1 (p c) -> p c", c=KVW)
            s_row.inc(sync.dma_start(out=khalo[:].rearrange("p a t -> p (a t)"), in_=src[:, 0:4 * 512]), 16)
            s_row.inc(sync.dma_start(out=vhalo[:].rearrange("p a t -> p (a t)"), in_=src[:, 4 * 512:]), 16)
        block.sync(f_e2br)
        E2B_R = s_row.n

        od = attention(3, False, pd, pa)
        _ck((l, "attn", 3))
        wo_c1(3, od)
        _escope(f"L{l}_win23")
        _scope(f"L{l}_halo")

        def f_gate0(t, rt=E2B_R):
            s_row.wait_at(t, rt)
        block.tensor(f_gate0)
        od = attention(0, True, pd, pa)
        _ck((l, "attn", 0))
        wo_c1(0, od)
        _ck((l, "c1pre"))

        def f_e3_st(v):
            s_dve.wait(v)
            inst = None
            for e in range(NCH):
                inst = v.tensor_copy(recv9[:, e:e + 1], stream[:, e, HALO + T - 1:HALO + T])
            s_dve.inc(inst)
        block.vector(f_e3_st)
        ag_event(f"e3_{l}", recv9[:, 0:8], recv9[:, 0:8])

        def f_e3_fix(v):
            s_row.wait(v)
            s_dve.inc(v.tensor_scalar_mul(small[:, 40:48], recv9[:, 0:8], small[:, 24:25]))
            s_dve.wait(v)
            inst = None
            for e in range(NCH):
                inst = v.tensor_copy(stream[:, e, 0:1], small[:, 40 + e:41 + e])
            s_dve.inc(inst)
        block.vector(f_e3_fix)
        _ck((l, "c1"))
        _escope(f"L{l}_halo")
        _scope(f"L{l}_ln4")

        def shsrc(e, tti):
            off = (0 if e >= 4 else 1) + tti * TT
            return stream[:, e, off:off + TT]

        def fw4(eng):
            s_dve.wait(eng)
        ln_stats(shsrc, fw4, pbk[2], pbk[4], (xb2, xb2b), (sq2, sq2b))
        pk_stats()
        rep4 = ln_rep()

        _escope(f"L{l}_ln4")
        _scope(f"L{l}_w1w2")
        for tti2 in range(2):
            d0g = s_dve.n
            a0g = s_act.n
            mulc = []

            def f_gbv(v, tti2=tti2, rep4=rep4, a0g=a0g):
                j = 0
                for e in range(NCH):
                    for sub in range(2):
                        tti = tti2 * 2 + sub
                        g = gen2[j % 2]
                        if j == 0:
                            s_row.wait_at(v, rep4)
                            s_act.wait_at(v, a0g)
                        if j >= 2:
                            s_act.wait_at(v, a0g + j - 1)
                        s_dve.inc(v.tensor_sub(g[:], shsrc(e, tti),
                                               m_rep[:, tti * TT:(tti + 1) * TT]))
                        if tti == 2 and e >= 4:
                            s_dve.wait(v)
                            s_dve.inc(v.tensor_sub(g[:, 0:1], ckeep[:, e:e + 1],
                                                   m_rep[:, tti * TT:tti * TT + 1]))
                        s_dve.wait(v)
                        s_dve.inc(v.tensor_mul(g[:], g[:], r_rep[:, tti * TT:(tti + 1) * TT]))
                        mulc.append(s_dve.n)
                        j += 1
            block.vector(f_gbv)

            def f_gbs(a, mulc=mulc, a0g=a0g):
                j = 0
                for e in range(NCH):
                    for sub in range(2):
                        s_dve.wait_at(a, mulc[j])
                        s_act.inc(a.activation(gbt[:, e, sub * TT:(sub + 1) * TT],
                                               gen2[j % 2][:], AF.Identity,
                                               bias=lsm[:, 60 + e:61 + e],
                                               scale=lsm[:, 52 + e:53 + e]))
                        j += 1
            block.scalar(f_gbs)
            assert s_act.n == a0g + 16

            if tti2 == 0:
                def f_sv(v, dprev=s_dve.n):
                    s_dve.wait_at(v, dprev)
                    s_gp.wait(v)
                    inst = None
                    for e in range(4, NCH):
                        inst = v.tensor_copy(ckeep[:, e:e + 1], stream[:, e, 1024:1025])
                    s_dve.inc(inst)
                block.vector(f_sv)
            GB_T = s_act.n
            _ck((l, "g", tti2))
            for kh in range(2):
                hks = HKS[kh * 11:(kh + 1) * 11]
                c0 = kh * 11 * 128
                ncols = min(11 * 128, HID - c0)
                pe0 = s_mm.n
                a0 = s_act.n
                d0 = s_dve.n
                w1l_t = []

                ki_slot = [(w1_g[0] + ki) % 2 for ki in range(11)]
                w1_g[0] += 11

                def f_gld(g, l=l, c0=c0, ncols=ncols, hks=hks, pe0=pe0, mprev=s_mm.n,
                          ams=tuple(w1am_s), ki_slot=ki_slot,
                          fresh=(tti2 == 0 and kh == 0)):
                    def w1load(ki, hk0, hkn):
                        slot = ki_slot[ki]
                        slsem = s_w1[slot]
                        wsl = w1t[slot]
                        if ki >= 2:
                            s_mm.wait_at(g, pe0 + 2 * ki - 2)
                        elif fresh:
                            s_mm.wait_at(g, mprev)
                        else:
                            s_mm.wait_at(g, ams[slot])
                        slsem.inc(g.dma_start(out=wsl[:, :, 0:hkn],
                                              in_=w1_i[l][:, hk0:hk0 + hkn].rearrange("(c p) e -> p c e", p=128)), 16)
                        slsem.inc(g.dma_start(out=wsl[:, :, 128:128 + hkn],
                                              in_=w1_i[l][:, HID + hk0:HID + hk0 + hkn].rearrange("(c p) e -> p c e", p=128)), 16)
                        w1l_t.append((slsem, slsem.n, wsl))
                    for ki in (0, 1):
                        w1load(ki, hks[ki][0], hks[ki][1])
                    s_mm.wait_at(g, mprev)
                    nfull = ncols // 128
                    rem = ncols - nfull * 128
                    s_w2.inc(g.dma_start(out=w2t[:, 0:nfull, :],
                                         in_=w2_i[l][c0:c0 + nfull * 128, :].rearrange("(c p) e -> p c e", p=128)), 16)
                    if rem:
                        s_w2.inc(g.dma_start(out=w2t[0:rem, nfull, :],
                                             in_=w2_i[l][c0 + nfull * 128:c0 + ncols, :]), 16)
                    for ki in range(2, 11):
                        w1load(ki, hks[ki][0], hks[ki][1])
                block.gpsimd(f_gld)
                W2L_T = s_w2.n

                ABK = (pbk[0], pbk[1], pbk[6])
                GBK = (pbk[2], pbk[3], pbk[7])

                def f_ams(t, hks=hks, pe0=pe0, a0=a0, d0=d0, gb=GB_T, w1l_t=w1l_t,
                          gs=tuple(gel_s), fs=tuple(f2_s)):
                    for ki, (hk0, hkn) in enumerate(hks):
                        slsem, w1l, wsl = w1l_t[ki]
                        for half2 in range(2):
                            j = 2 * ki + half2
                            if half2 == 0:
                                slsem.wait_at(t, w1l)
                                if ki == 0:
                                    s_act.wait_at(t, gb)
                            if j >= 3:
                                s_act.wait_at(t, a0 + j - 2)
                                s_dve.wait_at(t, d0 + j - 2)
                            else:
                                s_act.wait_at(t, gs[j % 3])
                                s_dve.wait_at(t, fs[j % 3])
                            hc = slice(half2 * 512, (half2 + 1) * 512)
                            for dch in range(NCH):
                                t.matmul(ABK[j % 3][0:hkn, :], wsl[:, dch, 0:hkn],
                                         gbt[:, dch, hc], start=(dch == 0), stop=(dch == NCH - 1))
                            inst = None
                            for dch in range(NCH):
                                inst = t.matmul(GBK[j % 3][0:hkn, :], wsl[:, dch, 128:128 + hkn],
                                                gbt[:, dch, hc], start=(dch == 0), stop=(dch == NCH - 1))
                            s_mm.inc(inst)
                block.tensor(f_ams)
                assert s_mm.n == pe0 + 22

                def f_gels(a, hks=hks, kh=kh, pe0=pe0, d0=d0,
                           f2p=(f2_cnt[0], f2_cnt[1])):
                    for ki, (hk0, hkn) in enumerate(hks):
                        for half2 in range(2):
                            j = 2 * ki + half2
                            s_mm.wait_at(a, pe0 + j + 1)
                            if ki == 0:
                                s_dve.wait_at(a, f2p[half2])
                            else:
                                s_dve.wait_at(a, d0 + j - 1)
                            bcol = 22 + kh * 11 + ki
                            sqh = (xb2, sq2)[half2]
                            s_act.inc(a.activation(sqh[0:hkn, :], GBK[j % 3][0:hkn, :],
                                                   AF.Gelu, bias=lsm[0:hkn, bcol:bcol + 1]))
                block.scalar(f_gels)
                assert s_act.n == a0 + 22

                def f_f2s(v, hks=hks, kh=kh, a0=a0):
                    for ki, (hk0, hkn) in enumerate(hks):
                        for half2 in range(2):
                            j = 2 * ki + half2
                            s_act.wait_at(v, a0 + j + 1)
                            acol = kh * 11 + ki
                            sqh = (xb2, sq2)[half2]
                            s_dve.inc(v.scalar_tensor_tensor(
                                f2b[0:hkn, ki, half2 * 512:(half2 + 1) * 512],
                                ABK[j % 3][0:hkn, :], lsm[0:hkn, acol:acol + 1],
                                sqh[0:hkn, :], AL.add, AL.mult))
                block.vector(f_f2s)
                assert s_dve.n == d0 + 22
                f2_cnt[0] = d0 + 21
                f2_cnt[1] = d0 + 22
                for _s in range(3):
                    _lastj = max(j2 for j2 in range(22) if j2 % 3 == _s)
                    gel_s[_s] = a0 + _lastj + 1
                    f2_s[_s] = d0 + _lastj + 1
                for _ki in range(11):
                    w1am_s[ki_slot[_ki]] = pe0 + 2 * _ki + 2
                F2_T = s_dve.n

                pec = s_mm.n
                dc0 = s_dve.n

                def f_c2ms(t, hks=hks, pec=pec, dc0=dc0, f2t=F2_T, w2l=W2L_T):
                    for idx in range(16):
                        e, half2 = idx // 2, idx % 2
                        if idx == 0:
                            s_dve.wait_at(t, f2t)
                            s_w2.wait_at(t, w2l)
                        if idx >= 2:
                            s_dve.wait_at(t, dc0 + idx - 1)
                        inst = None
                        for ki, (hk0, hkn) in enumerate(hks):
                            inst = t.matmul(pbk[4 + (idx % 2)][:, :],
                                            w2t[0:hkn, ki, e * 128:(e + 1) * 128],
                                            f2b[0:hkn, ki, half2 * 512:(half2 + 1) * 512],
                                            start=(ki == 0), stop=(ki == len(hks) - 1))
                        s_mm.inc(inst)
                block.tensor(f_c2ms)
                assert s_mm.n == pec + 16

                def f_c2es(v, tti2=tti2, kh=kh, pec=pec):
                    for idx in range(16):
                        e, half2 = idx // 2, idx % 2
                        s_mm.wait_at(v, pec + idx + 1)
                        csl = tok(tti2 * 1024 + half2 * 512, TT)
                        if kh == 0:
                            s_dve.inc(v.scalar_tensor_tensor(stream[:, e, csl], pbk[4 + (idx % 2)][:, :],
                                                             lsm[:, 44 + e:45 + e],
                                                             stream[:, e, csl], AL.add, AL.add))
                        else:
                            s_dve.inc(v.tensor_add(stream[:, e, csl], stream[:, e, csl],
                                                   pbk[4 + (idx % 2)][:, :]))
                block.vector(f_c2es)
                assert s_dve.n == dc0 + 16
        _escope(f"L{l}_w1w2")

    ffn_layer(0)
    _ck((0, "c2"))
    ffn_layer(1)
    _ck((1, "c2"))

    _scope("ln5")
    def ln5():
        def fw(eng):
            s_dve.wait(eng)
        ln_stats(lambda e, tti: stream[:, e, tok(tti * TT, TT)], fw, pbk[2], pbk[4], (xb2, xb2b), (sq2, sq2b))
        pk_stats()
        rep5 = ln_rep()

        x15 = [scratch[:, 8192:10240], scratch[:, 10240:12288]]
        d0 = s_dve.n
        g0 = s_gp.n
        io0 = (s_io[0].n, s_io[1].n)

        def sub_cnt(e):
            return d0 + (2 * e if e >= 1 else 1)

        def add_cnt(e):
            return d0 + (2 * e + 3 if e <= 6 else 16)

        def f_lds(sync, rep5=rep5):
            for e in range(NCH):
                if e == 0:
                    s_st.wait_at(sync, X1_ST)
                if e >= 2:
                    s_dve.wait_at(sync, add_cnt(e - 2))
                s_io[e % 2].inc(sync.dma_start(
                    out=x15[e % 2][:],
                    in_=x1_sc.rearrange("(c p) t -> p c t", p=128)[:, e, :]), 16)
        block.sync(f_lds)

        def f_v(v, rep5=rep5):
            for e in range(NCH):
                if e == 0:
                    s_row.wait_at(v, rep5)
                s_dve.inc(v.tensor_sub(stream[:, e, HALO:], stream[:, e, HALO:], m_rep[:]))
                if e >= 1:
                    ae = e - 1
                    s_gp.wait_at(v, g0 + ae + 1)
                    s_io[ae % 2].wait_at(v, io0[ae % 2] + 16 * (ae // 2 + 1))
                    s_dve.inc(v.tensor_add(stream[:, ae, HALO:], stream[:, ae, HALO:],
                                           x15[ae % 2][:]))
            s_gp.wait_at(v, g0 + 8)
            s_io[1].wait_at(v, io0[1] + 64)
            s_dve.inc(v.tensor_add(stream[:, 7, HALO:], stream[:, 7, HALO:], x15[1][:]))
        block.vector(f_v)
        assert s_dve.n == d0 + 16

        def f_g(g):
            for e in range(NCH):
                s_dve.wait_at(g, sub_cnt(e))
                s_gp.inc(g.tensor_mul(stream[:, e, HALO:], stream[:, e, HALO:], r_rep[:]))
        block.gpsimd(f_g)
        assert s_gp.n == g0 + 8

        def f_outs(sync):
            for e in range(NCH):
                s_dve.wait_at(sync, add_cnt(e))
                s_st.inc(sync.dma_start(
                    out=out_x2.rearrange("(c p) t -> p c t", p=128)[:, e, :],
                    in_=stream[:, e, HALO:]), 16)
        block.sync(f_outs)

        def f_fin(sync):
            s_st.wait(sync)
            s_cci.wait(sync)
        block.sync(f_fin)
    ln5()

    _escope("ln5")
    nc._dbg = dict(stream=stream, states=states, small=small, pk=pk, lsm=lsm,
                   qrot=qrot, krot=krot, vtm=vtm, ptil=ptil, o_b=o_b, gbt=gbt,
                   cbw=cbw, recv9=recv9, bfpool=bfpool, f2b=f2b, rnq=rnq, rnk=rnk)
    ctx.close()
    return nc


def _host_prep(inputs, core):
    b, half = core // 2, core % 2
    off = half * T
    x = np.asarray(inputs["L0_attn_x"])
    xm = np.zeros((D, 1 + T), np.float32)
    xm[:, 1:] = x[b, off:off + T].T
    if half == 1:
        xm[:, 0] = x[b, off - 1]
    iot = (np.arange(T, dtype=np.float32) + 1.0).reshape(128, 16)

    n_full = 4096
    DH = 64
    half_idx = np.arange(0, DH, 2, dtype=np.float64)
    inv_freq = 1.0 / 10000.0 ** (half_idx / DH)
    tpos = np.arange(off, off + T, dtype=np.float64)
    freqs = tpos[:, None] * inv_freq[None, :]
    freqs = np.concatenate([freqs, freqs], 1)
    base = (half_idx + 0.4 * DH) / (1.4 * DH)
    power = (tpos - n_full // 2) / 256.0
    scale = base[None, :] ** power[:, None]
    scale = np.concatenate([scale, scale], 1)
    cosv = np.cos(freqs)
    sinv = np.sin(freqs)
    sgn = np.concatenate([-np.ones(32), np.ones(32)])[None, :]
    cq = (cosv * scale).T.astype(np.float32)
    sq = (sinv * scale * sgn).T.astype(np.float32)
    ck = (cosv / scale).T.astype(np.float32)
    sk = (sinv / scale * sgn).T.astype(np.float32)
    cstab = np.stack([np.tile(cq, (2, 1)), np.tile(sq, (2, 1)),
                      np.tile(ck, (2, 1)), np.tile(sk, (2, 1))])

    jj = np.arange(128)[:, None]
    ii = np.arange(128)[None, :]
    utmask = np.where(jj <= ii, 0.0, -1e30).astype(ml_dtypes.bfloat16)
    ltmask = np.where(jj >= ii, 0.0, -1e30).astype(ml_dtypes.bfloat16)
    idmat = np.eye(128).astype(ml_dtypes.bfloat16)

    qs = np.asarray(inputs["L0_ffn_q_scale"], np.float32)
    ks = np.asarray(inputs["L0_ffn_k_scale"], np.float32)
    qkpat = np.zeros((2, 8, 128, 128), np.float32)
    for l in range(2):
        for cc in range(8):
            sv = qs[l] if cc < 4 else ks[l]
            for mh in range(2):
                krow = 32 * (cc % 4) + mh
                qkpat[l, cc, krow, mh * 64:(mh + 1) * 64] = sv

    b1 = np.asarray(inputs["L0_ffn_b1"], np.float32)
    b1pad = np.zeros((2, 44, 128), np.float32)
    for l in range(2):
        for ki, (h0, hn) in enumerate(HKS):
            b1pad[l, ki, :hn] = b1[l, h0:h0 + hn]
            b1pad[l, 22 + ki, :hn] = b1[l, HID + h0:HID + h0 + hn]
    b1p = b1pad.transpose(0, 2, 1)

    im = {
        "x_fm": np.ascontiguousarray(xm),
        "WinT": np.ascontiguousarray(np.asarray(inputs["L0_attn_W_in"], np.float32).T),
        "WoutT": np.ascontiguousarray(np.asarray(inputs["L0_attn_W_out"], np.float32).T),
        "b_inp": np.ascontiguousarray(np.asarray(inputs["L0_attn_b_in"], np.float32).reshape(NCH, 128).T),
        "b_outp": np.ascontiguousarray(np.asarray(inputs["L0_attn_b_out"], np.float32).reshape(NCH, 128).T),
        "scal": np.array([[float(inputs["L0_attn_decay"]), float(inputs["L0_attn_mix"]), 0, 0, 0, 0, 0, 0]], np.float32),
        "iotapk": np.ascontiguousarray(iot),
        "rolef": np.full((128, 1), float(half), np.float32),
        "rbias": np.full((128, 1), 0.0 if half == 1 else NEG, np.float32),
        "sel": np.array([[(core - 1) % 8]], np.int32),
        "wqkv": np.ascontiguousarray(np.asarray(inputs["L0_ffn_wqkv"], np.float32)),
        "wo_i": np.ascontiguousarray(np.asarray(inputs["L0_ffn_wo"], np.float32)),
        "w1_i": np.ascontiguousarray(np.asarray(inputs["L0_ffn_w1"], np.float32)),
        "w2_i": np.ascontiguousarray(np.asarray(inputs["L0_ffn_w2"], np.float32)),
        "b1p": np.ascontiguousarray(b1p),
        "b2p": np.ascontiguousarray(np.stack([np.asarray(inputs["L0_ffn_b2"], np.float32)[l].reshape(NCH, 128).T for l in range(2)])),
        "lgp": np.ascontiguousarray(np.stack([np.asarray(inputs["L0_ffn_ln_g"], np.float32)[l].reshape(NCH, 128).T for l in range(2)])),
        "lbp": np.ascontiguousarray(np.stack([np.asarray(inputs["L0_ffn_ln_b"], np.float32)[l].reshape(NCH, 128).T for l in range(2)])),
        "qsv": np.ascontiguousarray(qs[:, None, :]),
        "ksv": np.ascontiguousarray(ks[:, None, :]),
        "qkpat": np.ascontiguousarray(qkpat),
        "cstab4": np.ascontiguousarray(cstab.astype(np.float32)),
        "idmi": np.ascontiguousarray(idmat),
        "utm": np.ascontiguousarray(utmask),
        "ltm": np.ascontiguousarray(ltmask),
    }
    return im


def kernel(**inputs):
    if "nc" not in _cache:
        _cache["nc"] = build_nc()
    nc = _cache["nc"]
    in_maps = [_host_prep(inputs, c) for c in range(8)]
    res = run_bass_kernel_spmd(nc, in_maps, core_ids=list(range(8)))
    out = np.zeros((4, 4096, D), np.float32)
    for c in range(8):
        b, half = c // 2, c % 2
        out[b, half * T:(half + 1) * T] = res.results[c]["out_x2"].T
    return out
